# version 22
# baseline (speedup 1.0000x reference)
import os
import numpy as np
from contextlib import ExitStack

import concourse.bass as bass
import concourse.mybir as mybir
from concourse.bass_utils import run_bass_kernel_spmd

F32 = mybir.dt.float32
BF16 = mybir.dt.bfloat16
I16 = mybir.dt.int16
I32 = mybir.dt.int32
AF = mybir.ActivationFunctionType
ALU = mybir.AluOpType

D = 2048
TC = 2048
KC = 16
MLP = 16384
TSL = 512
NSL = TC // TSL
GK2 = 8
NGRP = 128 // GK2
NH = 16
CH = 128
NCH = TC // CH
CPS = TSL // CH
DQN = 2048

NCONST = 16
C_C1, C_C2, C_CK, C_BM2, C_BM6, C_EPS, C_FLAG, C_QS = range(8)

DEBUG_TAPS = int(os.environ.get("ATHENA_DEBUG_TAPS", "0"))
PHASE_LIM = int(os.environ.get("ATHENA_PHASES", "99"))
ENGS = ("sy", "gp", "ve", "sc", "te")


class Ctr:
    def __init__(self, ctx, nc, name):
        self.h = ctx.enter_context(nc.semaphore(name))
        self.n = 0


class B:

    POOL = {"sy": 24, "gp": 12, "sq": 16, "scp": 12}

    def __init__(self, nc, ctx):
        self.nc = nc
        self.th = {q: [] for q in ENGS}
        self.c = {q: Ctr(ctx, nc, "s_" + q) for q in ("ve", "sc", "te")}
        self.c["cc"] = Ctr(ctx, nc, "s_cc")
        self.c["ccw"] = Ctr(ctx, nc, "s_ccw")
        self.pool = {q: [Ctr(ctx, nc, "p_%s%d" % (q, i)) for i in range(n)]
                     for q, n in self.POOL.items()}
        self.rr = {q: 0 for q in self.POOL}

    def dma(self, q, out, in_, ctr=None):
        qq = ctr or q
        pool = self.pool[qq]
        s = pool[self.rr[qq]]
        self.rr[qq] = (self.rr[qq] + 1) % len(pool)
        if s.n > 0:
            self.th[q].append(lambda e, h=s.h, v=s.n: e.wait_ge(h, v))
        s.n += 16
        self.th[q].append(lambda e, o=out, i=in_, h=s.h:
                          e.dma_start(out=o, in_=i).then_inc(h, 16))
        return (s.h, s.n)

    def op(self, q, fn, inc=0):
        if inc:
            c = self.c[q]
            c.n += inc
            self.th[q].append(lambda e, f=fn, h=c.h, k=inc:
                              f(e).then_inc(h, k))
            return c.n
        self.th[q].append(fn)
        return None

    def wait(self, q, src, val):
        if val is None:
            return
        if isinstance(val, tuple):
            self.th[q].append(lambda e, h=val[0], v=val[1]: e.wait_ge(h, v))
            return
        if isinstance(val, list):
            for t_ in val:
                self.wait(q, src, t_)
            return
        if val <= 0:
            return
        self.th[q].append(lambda e, h=self.c[src].h, v=val: e.wait_ge(h, v))

    def qsnap(self, qq):
        return [(s.h, s.n) for s in self.pool[qq] if s.n > 0]

    def gate(self):
        snap = {q: self.c[q].n for q in ("ve", "sc", "te")}
        toks = [t for qq in self.POOL for t in self.qsnap(qq)]
        for q in ENGS:
            for r in ("ve", "sc", "te"):
                if r != q:
                    self.wait(q, r, snap[r])
            self.wait(q, None, toks)

    def finalize(self, block):
        block.sync(lambda e: [t(e) for t in self.th["sy"]])
        block.gpsimd(lambda e: [t(e) for t in self.th["gp"]])
        block.vector(lambda e: [t(e) for t in self.th["ve"]])
        block.scalar(lambda e: [t(e) for t in self.th["sc"]])
        block.tensor(lambda e: [t(e) for t in self.th["te"]])


def _nibble_perm(n):
    nch = n // 128
    pi = np.empty(n, dtype=np.int64)
    idx = np.arange(64)
    for ci in range(nch):
        pi[ci * 128:ci * 128 + 64] = 2 * (64 * ci + idx)
        pi[ci * 128 + 64:ci * 128 + 128] = 2 * (64 * ci + idx) + 1
    return pi


class _PhaseCutExc(Exception):
    pass


_PhaseCut = _PhaseCutExc()


def build_nc():
    nc = bass.Bass("TRN2", target_bir_lowering=False,
                   detect_race_conditions=False)

    xin = nc.dram_tensor("xin", [D, TC], F32, kind="ExternalInput").ap()
    wq = nc.dram_tensor("wq", [D, D], F32, kind="ExternalInput").ap()
    wk = nc.dram_tensor("wk", [D, D], F32, kind="ExternalInput").ap()
    wv = nc.dram_tensor("wv", [D, D], F32, kind="ExternalInput").ap()
    wo = nc.dram_tensor("wo", [D, D], F32, kind="ExternalInput").ap()
    gp_ = nc.dram_tensor("gp", [D // 2, DQN], I32, kind="ExternalInput").ap()
    up_ = nc.dram_tensor("up", [D // 2, DQN], I32, kind="ExternalInput").ap()
    dp_ = nc.dram_tensor("dp", [MLP // 16, D], I32, kind="ExternalInput").ap()
    gs = nc.dram_tensor("gs", [64, DQN], F32, kind="ExternalInput").ap()
    us_ = nc.dram_tensor("us", [64, DQN], F32, kind="ExternalInput").ap()
    ds = nc.dram_tensor("ds", [64, D], F32, kind="ExternalInput").ap()
    cts = nc.dram_tensor("cts", [128, NCONST], F32, kind="ExternalInput").ap()
    cti = nc.dram_tensor("cti", [128, 2], I32, kind="ExternalInput").ap()
    ctb = nc.dram_tensor("ctb", [128, 256], BF16, kind="ExternalInput").ap()
    yout = nc.dram_tensor("yout", [D, TC], F32, kind="ExternalOutput").ap()
    taps = {}
    if DEBUG_TAPS:
        for t in ("x1", "x2"):
            taps[t] = nc.dram_tensor("tap_" + t, [D, TC], F32,
                                     kind="ExternalOutput").ap()
    if DEBUG_TAPS >= 2:
        taps["dq"] = nc.dram_tensor("tap_dq", [NH, 128, TC], BF16,
                                    kind="ExternalOutput").ap()
        taps["da"] = nc.dram_tensor("tap_da", [NH, 128, TC], BF16,
                                    kind="ExternalOutput").ap()
        taps["dcc"] = nc.dram_tensor("tap_dcc", [128, NH * 128], F32,
                                     kind="ExternalOutput").ap()
        taps["dsin"] = nc.dram_tensor("tap_dsin", [128, NH * 128], BF16,
                                      kind="ExternalOutput").ap()
        taps["dwq"] = nc.dram_tensor("tap_dwq", [D, D], BF16,
                                     kind="ExternalOutput").ap()
        taps["dxn"] = nc.dram_tensor("tap_dxn", [128, KC * TSL], BF16,
                                     kind="ExternalOutput").ap()
        taps["dwk"] = nc.dram_tensor("tap_dwk", [D, D], BF16,
                                     kind="ExternalOutput").ap()

    sexg = nc.dram_tensor("sexg", [D, DQN], BF16, kind="Internal").ap()
    sexu = nc.dram_tensor("sexu", [D, DQN], BF16, kind="Internal").ap()
    sexd = nc.dram_tensor("sexd", [MLP // 8, D], BF16, kind="Internal").ap()
    cgs = nc.dram_tensor("cgs", [8, 128, KC, 256], BF16,
                         kind="Internal").ap()
    cus = nc.dram_tensor("cus", [8, 128, KC, 256], BF16,
                         kind="Internal").ap()
    cds_a = nc.dram_tensor("cds_a", [MLP // 16, D], BF16,
                           kind="Internal").ap()
    cds_b = nc.dram_tensor("cds_b", [MLP // 16, D], BF16,
                           kind="Internal").ap()
    cg = nc.dram_tensor("cg", [64, 128, KC, 256], BF16, kind="Internal",
                        addr_space="Shared").ap()
    cu = nc.dram_tensor("cu", [64, 128, KC, 256], BF16, kind="Internal",
                        addr_space="Shared").ap()
    cd_a = nc.dram_tensor("cd_a", [MLP // 2, D], BF16, kind="Internal",
                          addr_space="Shared").ap()
    cd_b = nc.dram_tensor("cd_b", [MLP // 2, D], BF16, kind="Internal",
                          addr_space="Shared").ap()
    wcq = nc.dram_tensor("wcq", [D, D], BF16, kind="Internal").ap()
    wck = nc.dram_tensor("wck", [D, D], BF16, kind="Internal").ap()
    wcv = nc.dram_tensor("wcv", [D, D], BF16, kind="Internal").ap()
    wco = nc.dram_tensor("wco", [D, D], BF16, kind="Internal").ap()
    qtd = nc.dram_tensor("qtd", [NH, 128, TC], BF16, kind="Internal").ap()
    atd = nc.dram_tensor("atd", [NH, 128, TC], BF16, kind="Internal").ap()
    ccin = nc.dram_tensor("ccin", [128, NH * 128], F32, kind="Internal")
    ccout = nc.dram_tensor("ccout", [256, NH * 128], F32, kind="Internal")

    ctx = ExitStack()
    with ctx:
        mstr = ctx.enter_context(nc.sbuf_tensor("mstr", [128, KC * TC], F32))
        ct = ctx.enter_context(nc.sbuf_tensor("ct", [128, NCONST], F32))
        cti_t = ctx.enter_context(nc.sbuf_tensor("cti_t", [128, 2], I32))
        cb = ctx.enter_context(nc.sbuf_tensor("cb", [128, 256], BF16))
        base = (nc.lookup_mloc(cb).addr + 512 + 31) & ~31
        LIMIT = 16512 + 212864

        def at(name, shape, dtype, off):
            nb = int(np.prod(shape[1:])) * mybir.dt.size(dtype)
            assert base + off + nb <= LIMIT, (name, base + off + nb, LIMIT)
            return nc.alloc_sbuf_tensor_at(name, shape, dtype,
                                           offset=base + off)

        dq_pt = [at("dq_pt0", [128, DQN], I16, 0),
                 at("dq_pt1", [128, DQN], I16, 4096)]
        dq_sx = [at("dq_sx0", [128, DQN], BF16, 8192),
                 at("dq_sx1", [128, DQN], BF16, 12288)]
        dq_u7 = [at("dq_u70", [128, DQN], I16, 16384),
                 at("dq_u71", [128, DQN], I16, 20480)]
        dq_sb = [at("dq_sb0", [128, DQN], I16, 24576),
                 at("dq_sb1", [128, DQN], I16, 28672)]
        dq_sg2 = [at("dq_sg20", [128, DQN], BF16, 24576),
                  at("dq_sg21", [128, DQN], BF16, 28672)]
        dq_a1 = [at("dq_a10", [128, DQN], BF16, 32768),
                 at("dq_a11", [128, DQN], BF16, 36864)]
        dq_a2 = [at("dq_a20", [128, DQN], BF16, 40960),
                 at("dq_a21", [128, DQN], BF16, 45056)]
        dq_a0 = [at("dq_a00", [128, DQN], BF16, 49152),
                 at("dq_a01", [128, DQN], BF16, 53248)]
        dq_wt = [at("dq_wt0", [128, DQN], BF16, 57344),
                 at("dq_wt1", [128, DQN], BF16, 61440)]
        cvf = at("cvf", [128, D], F32, 65536)
        cvb = at("cvb", [128, D], BF16, 73728)

        f_xn = at("f_xn", [128, KC * TSL], BF16, 0)
        f_int = at("f_int", [128, GK2 * TSL], BF16, 16384)
        f_wgu = [at("f_wgu%d" % i, [128, KC, 256], BF16, 24576 + 8192 * i)
                 for i in range(4)]
        f_wdn = [at("f_wdn%d" % i, [128, 512], BF16, 57344 + 1024 * i)
                 for i in range(16)]
        f_rstd = at("f_rstd", [128, TSL], F32, 73728)
        f_s32a = at("f_s32a", [128, TSL], F32, 75776)
        f_gel = [at("f_gel0", [128, TSL], BF16, 77824),
                 at("f_gel1", [128, TSL], BF16, 78848)]
        f_xb = f_gel[0]
        f_xq = f_gel[1]

        a_xn = f_xn
        a_qcs = at("a_qcs", [128, NH * TSL], BF16, 0)
        a_qf = at("a_qf", [128, NH * TSL], BF16, 0)
        a_kt = at("a_kt", [128, NH * TSL], BF16, 16384)
        a_mrg = at("a_mrg", [128, NH * TSL], BF16, 16384)
        a_ko = at("a_ko", [128, CPS * D], BF16, 32768)
        a_ccs = at("a_ccs", [128, NH * 128], F32, 32768)
        a_sin = at("a_sin", [128, NH * 128], BF16, 40960)
        a_vo = at("a_vo", [128, CPS * D], BF16, 49152)
        a_wp = [at("a_wp%d" % i, [128, 512], BF16, 65536 + 1024 * i)
                for i in range(4)]
        a_tq = [at("a_tq0", [128, TSL], BF16, 69632),
                at("a_tq1", [128, TSL], BF16, 70656),
                at("a_tq2", [128, TSL], BF16, 71680),
                at("a_tq3", [128, TSL], BF16, 72704)]
        a_st = at("a_st", [128, NH * 128], BF16, 71680)
        a_sc = [at("a_sc0", [128, 128], BF16, 75776),
                at("a_sc1", [128, 128], BF16, 76032)]
        a_or = [at("a_or0", [128, 128], BF16, 76288),
                at("a_or1", [128, 128], BF16, 76544)]

        ps = [ctx.enter_context(nc.psum_tensor("psb%d" % i, [128, 512], F32))
              for i in range(8)]

        block = ctx.enter_context(nc.Block())
        b = B(nc, ctx)

        def M(kc):
            return mstr[:, kc * TC:(kc + 1) * TC]

        b.dma("sy", ct[:, :], cts[:, :])
        b.dma("sy", cb[:, :], ctb[:, :])
        b.dma("sy", cti_t[:, :], cti[:, :])
        sy_consts = b.qsnap("sy")
        for kc in range(KC):
            b.dma("sy", M(kc), xin[kc * 128:(kc + 1) * 128, :])
        sy_master = b.qsnap("sy")

        def emit_exp(which, ci):
            dst, srcs, w = ((sexg, gs, DQN), (sexu, us_, DQN),
                            (sexd, ds, D))[which]
            toks = []
            for h2 in range(2):
                d_ = dst[ci * 128 + 64 * h2:ci * 128 + 64 * h2 + 64, :]
                d_ = d_.rearrange("(bb i) n -> bb i n", bb=4, i=16)
                s_ = srcs[4 * ci:4 * ci + 4, :].unsqueeze(1)
                s_ = s_.broadcast_to([4, 16, w])
                toks.append(b.dma("gp", d_, s_))
            return toks
        EXP_JIT = int(os.environ.get("ATHENA_EXP_JIT", "1"))
        gp_exp = None
        if not EXP_JIT:
            for which, nch in ((0, KC), (1, KC), (2, KC)):
                for ci in range(nch):
                    exp_done_pre = emit_exp(which, ci)
            gp_exp = b.qsnap("gp")

        tiles = []
        for mat in (0, 1):
            for kc in range(KC):
                tiles.append(("gu", mat, kc, 0))
        for kc2 in range(KC):
            tiles.append(("dn", kc2, 0, 0))
        NT = len(tiles)

        convs = []
        for (wi, wc_, scol) in ((wq, wcq, C_QS), (wk, wck, None),
                                (wv, wcv, None), (wo, wco, None)):
            for kc in range(KC):
                convs.append((wi, wc_, kc, scol))

        b.wait("gp", "sy", sy_consts)
        b.wait("ve", "sy", sy_consts)
        b.wait("sc", "sy", sy_consts)
        t_wt = [None] * NT
        t_cw = [None] * NT
        conv_ld = [None] * len(convs)
        conv_cv = [None] * len(convs)
        conv_st = [None] * len(convs)
        ci_load = ci_conv = ci_store = 0

        dv_A = [None] * NT
        ac_rel = [None] * NT

        def emit_A(t):
            kind, aa, bb_, cc_ = tiles[t]
            i = t % 2
            b.wait("ve", "gp", gq_after[t])
            b.wait("ve", "sy", sx_after[t])
            b.op("ve", lambda e, i=i: e.tensor_single_scalar(
                dq_u7[i][:, :], dq_pt[i][:, :], cti_t[:, 0:1],
                ALU.bitwise_and))
            dv_A[t] = b.op("ve", lambda e, i=i: e.tensor_single_scalar(
                dq_sb[i][:, :], dq_pt[i][:, :], cti_t[:, 1:2],
                ALU.bitwise_and), inc=1)

        def emit_B(t):
            kind, aa, bb_, cc_ = tiles[t]
            i = t % 2
            b.wait("ve", "sc", ac_rel[t])
            if t >= 2:
                b.wait("ve", "sy", t_cw[t - 2])
            b.op("ve", lambda e, i=i: e.tensor_add(
                dq_a0[i][:, :], dq_a0[i][:, :], dq_a1[i][:, :]))
            b.op("ve", lambda e, i=i: e.tensor_add(
                dq_a0[i][:, :], dq_a0[i][:, :], dq_a2[i][:, :]))
            b.op("ve", lambda e, i=i: e.tensor_scalar(
                dq_sg2[i][:, :], dq_sb[i][:, :], ct[:, C_CK:C_CK + 1], 1.0,
                ALU.mult, ALU.add))
            b.op("ve", lambda e, i=i: e.tensor_mul(
                dq_sg2[i][:, :], dq_sg2[i][:, :], dq_a0[i][:, :]))
            t_wt[t] = b.op("ve", lambda e, i=i: e.tensor_mul(
                dq_wt[i][:, :], dq_sg2[i][:, :], dq_sx[i][:, :]), inc=1)

        def emit_W(t):
            kind, aa, bb_, cc_ = tiles[t]
            i = t % 2
            b.wait("sy", "ve", t_wt[t])
            if kind == "gu":
                mat, kc, ns = aa, bb_, cc_
                cch = cgs if mat == 0 else cus
                p0 = ns * (DQN // 256)
                dst = cch[p0:p0 + 8, :, kc, :].rearrange("q p n -> p q n")
                src2 = dq_wt[i][:, :].rearrange("p (q n) -> p q n", q=8)
                t_cw[t] = b.dma("sy", dst, src2)
            else:
                kc2 = aa
                half, r = (cds_a, kc2) if kc2 < 8 else (cds_b, kc2 - 8)
                t_cw[t] = b.dma("sy", half[r * 128:(r + 1) * 128, :],
                                dq_wt[i][:, :])

        gq_after = [None] * NT
        sx_after = [None] * NT
        exp_done = {}

        def emit_gather(seg_, full_, toks):
            b.wait("gp", None, toks)
            cw = b.c["ccw"]
            cw.n += 1
            b.th["gp"].append(
                lambda e, s=seg_, f=full_, h=cw.h: e.collective_compute(
                    "AllGather", ALU.bypass, ins=[s], outs=[f],
                    replica_groups=[[0, 1, 2, 3, 4, 5, 6, 7]],
                ).then_inc(h, 1))

        for t in range(NT):
            if t == 17:
                emit_gather(cgs, cg, [t_cw[j] for j in range(16)])
            if t == 33:
                emit_gather(cus, cu, [t_cw[j] for j in range(16, 32)])
            if t == 41:
                emit_gather(cds_a, cd_a, [t_cw[j] for j in range(32, 40)])
            kind, aa, bb_, cc_ = tiles[t]
            i = t % 2
            if t >= 2:
                b.wait("gp", "ve", dv_A[t - 2])
            if kind == "gu":
                mat, kc, ns = aa, bb_, cc_
                srcp = gp_ if mat == 0 else up_
                sxs = sexg if mat == 0 else sexu
                if ns == 0:
                    exp_done[(mat, kc)] = (emit_exp(mat, kc) if EXP_JIT
                                           else gp_exp)
                rows = srcp[kc * 64:(kc + 1) * 64, :]
                tp0 = b.dma("gp", dq_pt[i][0:64, :], rows)
                tp1 = b.dma("gp", dq_pt[i][64:128, :], rows)
                if t >= 2:
                    b.wait("sy", "ve", t_wt[t - 2])
                b.wait("sy", "gp", exp_done[(mat, kc)])
                sx_after[t] = b.dma(
                    "sy", dq_sx[i][:, :],
                    sxs[kc * 128:(kc + 1) * 128, :])
            else:
                kc2 = aa
                if kc2 % 16 == 0:
                    for cj in range(kc2, kc2 + 16):
                        exp_done[(2, cj)] = (emit_exp(2, cj) if EXP_JIT
                                             else gp_exp)
                rows = dp_[kc2 * 64:(kc2 + 1) * 64, :]
                tp0 = b.dma("gp", dq_pt[i][0:64, :], rows)
                tp1 = b.dma("gp", dq_pt[i][64:128, :], rows)
                if t >= 2:
                    b.wait("sy", "ve", t_wt[t - 2])
                b.wait("sy", "gp", exp_done[(2, kc2)])
                sx_after[t] = b.dma("sy", dq_sx[i][:, :],
                                    sexd[kc2 * 128:(kc2 + 1) * 128, :])
            key = (mat, kc) if kind == "gu" else (2, aa)
            gq_after[t] = [tp0, tp1] + list(exp_done[key])

            for _ in range(3):
                if ci_conv < ci_load:
                    wi, wc_, kc_c, scol = convs[ci_conv]
                    b.wait("sc", None, conv_ld[ci_conv])
                    if ci_conv >= 1:
                        b.wait("sc", None, conv_st[ci_conv - 1])
                    if scol is None:
                        conv_cv[ci_conv] = b.op(
                            "sc", lambda e: e.copy(cvb[:, :], cvf[:, :]),
                            inc=1)
                    else:
                        conv_cv[ci_conv] = b.op(
                            "sc", lambda e, scol=scol: e.activation(
                                cvb[:, :], cvf[:, :], AF.Copy,
                                scale=ct[:, scol:scol + 1]), inc=1)
                    ci_conv += 1
                if ci_store < ci_conv:
                    wi, wc_, kc_c, scol = convs[ci_store]
                    b.wait("sy", "sc", conv_cv[ci_store])
                    conv_st[ci_store] = b.dma(
                        "sy", wc_[kc_c * 128:(kc_c + 1) * 128, :], cvb[:, :])
                    ci_store += 1
                if ci_load < len(convs) and ci_load <= ci_conv:
                    wi, wc_, kc_c, scol = convs[ci_load]
                    conv_ld[ci_load] = b.dma(
                        "sc", cvf[:, :], wi[kc_c * 128:(kc_c + 1) * 128, :],
                        ctr="scp")
                    ci_load += 1

            emit_A(t)

            b.wait("sc", "ve", dv_A[t])
            b.op("sc", lambda e, i=i: e.activation(
                dq_a0[i][:, :], dq_u7[i][:, :], AF.Relu,
                scale=ct[:, C_C1:C_C1 + 1]))
            b.op("sc", lambda e, i=i: e.activation(
                dq_a1[i][:, :], dq_u7[i][:, :], AF.Relu,
                bias=ct[:, C_BM2:C_BM2 + 1], scale=ct[:, C_C1:C_C1 + 1]))
            ac_rel[t] = b.op("sc", lambda e, i=i: e.activation(
                dq_a2[i][:, :], dq_u7[i][:, :], AF.Relu,
                bias=ct[:, C_BM6:C_BM6 + 1], scale=ct[:, C_C2:C_C2 + 1]),
                inc=1)

            if t >= 1:
                emit_B(t - 1)
                emit_W(t - 1)
        emit_B(NT - 1)
        emit_W(NT - 1)

        emit_gather(cds_b, cd_b, [t_cw[j] for j in range(40, NT)])

        while ci_conv < len(convs):
            wi, wc_, kc_c, scol = convs[ci_conv]
            b.wait("sc", None, conv_ld[ci_conv])
            b.wait("sc", None, conv_st[ci_conv - 1])
            if scol is None:
                conv_cv[ci_conv] = b.op(
                    "sc", lambda e: e.copy(cvb[:, :], cvf[:, :]), inc=1)
            else:
                conv_cv[ci_conv] = b.op(
                    "sc", lambda e, scol=scol: e.activation(
                        cvb[:, :], cvf[:, :], AF.Copy,
                        scale=ct[:, scol:scol + 1]), inc=1)
            ci_conv += 1
            b.wait("sy", "sc", conv_cv[ci_store])
            conv_st[ci_store] = b.dma(
                "sy", wc_[kc_c * 128:(kc_c + 1) * 128, :], cvb[:, :])
            ci_store += 1
        while ci_store < len(convs):
            wi, wc_, kc_c, scol = convs[ci_store]
            b.wait("sy", "sc", conv_cv[ci_store])
            conv_st[ci_store] = b.dma(
                "sy", wc_[kc_c * 128:(kc_c + 1) * 128, :], cvb[:, :])
            ci_store += 1
        assert ci_load == len(convs), ci_load

        b.wait("ve", "sy", sy_master)
        b.gate()

        def emit_ln(ts, xn_buf, sq_eng="sc"):
            t0 = ts * TSL
            pe_k = [None] * KC
            for kc in range(KC):
                if sq_eng == "sc":
                    xb_, xq_ = f_xb, f_xq
                    if kc >= 1:
                        b.wait("sc", "te", pe_k[kc - 1])
                    b.op("sc", lambda e, kc=kc, t0=t0, xb_=xb_: e.copy(
                        xb_[:, :], M(kc)[:, t0:t0 + TSL]))
                    ac_sq = b.op("sc", lambda e, xb_=xb_, xq_=xq_:
                                 e.activation(xq_[:, :], xb_[:, :],
                                              AF.Square), inc=1)
                else:
                    xb_ = f_xb if kc % 2 == 0 else a_tq[2]
                    xq_ = f_xq if kc % 2 == 0 else a_tq[3]
                    if kc >= 2:
                        b.wait("ve", "te", pe_k[kc - 2])
                    b.op("ve", lambda e, kc=kc, t0=t0, xb_=xb_:
                         e.tensor_copy(xb_[:, :], M(kc)[:, t0:t0 + TSL]))
                    ac_sq = b.op("ve", lambda e, xb_=xb_, xq_=xq_:
                                 e.tensor_mul(xq_[:, :], xb_[:, :],
                                              xb_[:, :]), inc=1)
                b.wait("te", sq_eng, ac_sq)
                b.op("te", lambda e, kc=kc, xb_=xb_: e.matmul(
                    ps[6][:, 0:TSL], cb[:, 128:256], xb_[:, :],
                    start=(kc == 0), stop=(kc == KC - 1)))
                pe_k[kc] = b.op("te", lambda e, kc=kc, xq_=xq_: e.matmul(
                    ps[7][:, 0:TSL], cb[:, 128:256], xq_[:, :],
                    start=(kc == 0), stop=(kc == KC - 1)), inc=1)
            pe_last = pe_k[-1]

            b.wait("ve", "te", pe_last)
            b.op("ve", lambda e: e.tensor_single_scalar(
                f_rstd[:, :], ps[6][:, 0:TSL], 1.0 / D, ALU.mult))
            b.op("ve", lambda e: e.tensor_single_scalar(
                f_s32a[:, :], ps[7][:, 0:TSL], 1.0 / D, ALU.mult))
            b.op("ve", lambda e: e.tensor_mul(
                f_rstd[:, :], f_rstd[:, :], f_rstd[:, :]))
            dv_var = b.op("ve", lambda e: e.tensor_sub(
                f_s32a[:, :], f_s32a[:, :], f_rstd[:, :]), inc=1)
            b.wait("sc", "ve", dv_var)
            ac_sd = b.op("sc", lambda e: e.activation(
                f_rstd[:, :], f_s32a[:, :], AF.Sqrt,
                bias=ct[:, C_EPS:C_EPS + 1]), inc=1)
            b.wait("ve", "sc", ac_sd)
            b.op("ve", lambda e: e.reciprocal(f_rstd[:, :], f_rstd[:, :]))
            dv_xn_k = []
            for kc in range(KC):
                b.op("ve", lambda e, kc=kc, t0=t0: e.scalar_tensor_tensor(
                    f_s32a[:, :], ps[6][:, 0:TSL], -1.0 / D,
                    M(kc)[:, t0:t0 + TSL], ALU.mult, ALU.add))
                dv_xn_k.append(b.op(
                    "ve", lambda e, kc=kc, xb=xn_buf: e.tensor_mul(
                        xb[:, kc * TSL:(kc + 1) * TSL], f_s32a[:, :],
                        f_rstd[:, :]), inc=1))
            return dv_xn_k

        def emit_ffn(dn_gate=False, out_stream=False):
            dn_slot = [0]
            dn_used = [None] * 16
            gel_ac = [None, None]
            int_dv = [None, None]
            gu_pe = [None] * 4
            dn_bank_dv = [None, None]
            pe_dn_all = None
            for ts in range(NSL):
                t0 = ts * TSL
                if ts == 0:
                    b.gate()
                else:
                    b.wait("sc", "ve", int_dv[0])
                    b.wait("sc", "ve", int_dv[1])
                dv_xn = emit_ln(ts, f_xn)

                for g in range(NGRP):
                    for pr in range(4):
                        p = g * 4 + pr
                        sl = (p % 2) * 2
                        b.wait("sy", "te", gu_pe[sl])
                        tk_g = b.dma("sy", f_wgu[sl][:, :, :], cg[p])
                        b.wait("gp", "te", gu_pe[sl + 1])
                        tk_u = b.dma("gp", f_wgu[sl + 1][:, :, :], cu[p])
                        sy_pair = [tk_g, tk_u]

                        b.wait("te", "sy", sy_pair)
                        for half in range(2):
                            for sub in range(2):
                                bank = half * 2 + sub
                                if half == 0:
                                    b.wait("te", "sc", gel_ac[sub])
                                else:
                                    b.wait("te", "ve", int_dv[sub])
                                pe_v = None
                                for kc in range(KC):
                                    if g == 0 and pr == 0 and half == 0 \
                                            and sub == 0:
                                        b.wait("te", "ve", dv_xn[kc])
                                    last = (kc == KC - 1)
                                    pe_v = b.op(
                                        "te",
                                        lambda e, hf=half, sb2=sub, kc=kc,
                                        bank=bank, sl=sl: e.matmul(
                                            ps[bank][:, 0:TSL],
                                            f_wgu[sl + hf][
                                                :, kc,
                                                sb2 * 128:sb2 * 128 + 128],
                                            f_xn[:,
                                                 kc * TSL:(kc + 1) * TSL],
                                            start=(kc == 0), stop=last),
                                        inc=(1 if last else 0))
                                if half == 0:
                                    b.wait("sc", "te", pe_v)
                                    gel_ac[sub] = b.op(
                                        "sc", lambda e, sb2=sub: e.activation(
                                            f_gel[sb2][:, :],
                                            ps[sb2][:, 0:TSL],
                                            AF.Gelu_apprx_tanh), inc=1)
                                else:
                                    b.wait("ve", "te", pe_v)
                                    b.wait("ve", "sc", gel_ac[sub])
                                    if pe_dn_all is not None and g >= 1 \
                                            and pr == 0:
                                        b.wait("ve", "te", pe_dn_all)
                                    kk = pr * 2 + sub
                                    int_dv[sub] = b.op(
                                        "ve", lambda e, sb2=sub, kk=kk:
                                        e.tensor_mul(
                                            f_int[:, kk * TSL:(kk + 1) * TSL],
                                            f_gel[sb2][:, :],
                                            ps[2 + sb2][:, 0:TSL]), inc=1)
                            gu_pe[sl + half] = b.c["te"].n
                    b.wait("te", "ve", int_dv[1])
                    if dn_gate and ts == 0 and g <= 1:
                        b.wait("sc", "ccw", 3 + g)
                    qslots = [None] * 4
                    for eq in range(4):
                        qslots[eq] = []
                        dn_toks = []
                        cdh = cd_a if g % 2 == 0 else cd_b
                        rb = (g // 2) * 1024
                        for j in range(GK2):
                            slq = dn_slot[0]
                            b.wait("sc", "te", dn_used[slq])
                            dn_toks.append(b.dma(
                                "sc", f_wdn[slq][:, :],
                                cdh[rb + j * 128:rb + (j + 1) * 128,
                                    eq * 512:(eq + 1) * 512], ctr="scp"))
                            qslots[eq].append(slq)
                            dn_slot[0] = (slq + 1) % 16
                        b.wait("te", None, dn_toks)
                        for e4 in range(4):
                            e = eq * 4 + e4
                            bank = 4 + (e % 2)
                            b.wait("te", "ve", dn_bank_dv[e % 2])
                            pe_e = None
                            for j in range(GK2):
                                slq = qslots[eq][j]
                                last = (j == GK2 - 1)
                                pe_e = b.op(
                                    "te", lambda e_, slq=slq, e4=e4, j=j,
                                    bank=bank: e_.matmul(
                                        ps[bank][:, 0:TSL],
                                        f_wdn[slq][:, e4 * 128:
                                                   e4 * 128 + 128],
                                        f_int[:, j * TSL:(j + 1) * TSL],
                                        start=(j == 0), stop=last),
                                    inc=(1 if last else 0))
                            if e4 == 3:
                                for slq in qslots[eq]:
                                    dn_used[slq] = pe_e
                            b.wait("ve", "te", pe_e)
                            dn_bank_dv[e % 2] = b.op(
                                "ve", lambda e_, e=e, t0=t0, bank=bank:
                                e_.scalar_tensor_tensor(
                                    M(e)[:, t0:t0 + TSL], ps[bank][:, 0:TSL],
                                    0.5, M(e)[:, t0:t0 + TSL],
                                    ALU.mult, ALU.add), inc=1)
                    pe_dn_all = b.c["te"].n
                if out_stream:
                    b.wait("sy", "ve", dn_bank_dv[0])
                    b.wait("sy", "ve", dn_bank_dv[1])
                    for kc in range(KC):
                        b.dma("sy", yout[kc * 128:(kc + 1) * 128,
                                         t0:t0 + TSL],
                              M(kc)[:, t0:t0 + TSL])

        b.wait("sy", "ccw", 2)
        b.wait("gp", "ccw", 2)
        if PHASE_LIM >= 2:
            emit_ffn(dn_gate=True)
        if DEBUG_TAPS:
            b.gate()
            for kc in range(KC):
                b.dma("sy", taps["x1"][kc * 128:(kc + 1) * 128, :], M(kc))

        hdq = [None, None, None, None]
        wp_pe = [None] * 4
        st_upd_pe = {}
        st_ev_dv = {}
        sc_msk_dv = [None, None]
        out_ev_ac = [None, None]
        or_sy = [None, None]
        ccs_dv = {}
        for ts in (range(NSL) if PHASE_LIM >= 3 else []):
            t0 = ts * TSL
            if ts == 0:
                b.gate()
            b.wait("ve", None, [tk for tk in hdq if tk])
            dv_xn = emit_ln(ts, a_xn, sq_eng="ve")
            if DEBUG_TAPS >= 2 and ts == 0:
                b.wait("sy", "ve", dv_xn[-1])
                b.dma("sy", taps["dxn"][:, :], a_xn[:, :])

            b.wait("te", "ve", dv_xn[-1])
            for (wc_, is_q) in ((wcq, True), (wck, False)):
                ev_ac = [None] * 4
                for ep in range(4):
                    for kc in range(KC):
                        wsl = (ep * KC + kc) % 4
                        b.wait("sy", "te", wp_pe[wsl])
                        sy_w = b.dma(
                            "sy", a_wp[wsl][:, :],
                            wc_[kc * 128:(kc + 1) * 128,
                                ep * 512:(ep + 1) * 512], ctr="sq")
                        b.wait("te", None, sy_w)
                        if kc == 0:
                            for e4 in range(4):
                                b.wait("te", "sc", ev_ac[e4])
                        pe_v = None
                        for e4 in range(4):
                            last = (kc == KC - 1)
                            pe_v = b.op(
                                "te", lambda e, wsl=wsl, e4=e4, kc=kc:
                                e.matmul(
                                    ps[4 + e4][:, 0:TSL],
                                    a_wp[wsl][:, e4 * 128:e4 * 128 + 128],
                                    a_xn[:, kc * TSL:(kc + 1) * TSL],
                                    start=(kc == 0), stop=last),
                                inc=(1 if e4 == 3 else 0))
                        wp_pe[wsl] = b.c["te"].n
                    for e4 in range(4):
                        h = ep * 4 + e4
                        b.wait("sc", "te", b.c["te"].n)
                        if is_q:
                            tq = e4 % 4
                            b.wait("sc", "sy", hdq[tq])
                            ev_ac[e4] = b.op(
                                "sc", lambda e, tq=tq, e4=e4: e.copy(
                                    a_tq[tq][:, :], ps[4 + e4][:, 0:TSL]),
                                inc=1)
                            b.wait("sy", "sc", ev_ac[e4])
                            hdq[tq] = b.dma(
                                "sy", qtd[h][:, t0:t0 + TSL],
                                a_tq[tq][:, :])
                        else:
                            ev_ac[e4] = b.op(
                                "sc", lambda e, h=h, e4=e4: e.activation(
                                    a_kt[:, h * TSL:(h + 1) * TSL],
                                    ps[4 + e4][:, 0:TSL], AF.Sign), inc=1)
                for e4 in range(4):
                    b.wait("te", "sc", ev_ac[e4])

            for (wc_, abuf) in ((wck, a_ko), (wcv, a_vo)):
                ev_ac = [None] * 4
                for es in range(4):
                    for kc in range(KC):
                        wsl = (es * KC + kc) % 4
                        b.wait("sy", "te", wp_pe[wsl])
                        sy_w = b.dma(
                            "sy", a_wp[wsl][:, :],
                            wc_[kc * 128:(kc + 1) * 128,
                                es * 512:(es + 1) * 512], ctr="sq")
                        b.wait("te", None, sy_w)
                        if kc == 0:
                            for tc in range(4):
                                b.wait("te", "sc", ev_ac[tc])
                        pe_v = None
                        for tc in range(4):
                            last = (kc == KC - 1)
                            pe_v = b.op(
                                "te", lambda e, wsl=wsl, tc=tc, kc=kc:
                                e.matmul(
                                    ps[4 + tc][:, 0:TSL],
                                    a_xn[:, kc * TSL + tc * 128:
                                         kc * TSL + tc * 128 + 128],
                                    a_wp[wsl][:, :],
                                    start=(kc == 0), stop=last),
                                inc=(1 if tc == 3 else 0))
                        wp_pe[wsl] = b.c["te"].n
                    for tc in range(4):
                        b.wait("sc", "te", b.c["te"].n)
                        ev_ac[tc] = b.op(
                            "sc", lambda e, tc=tc, es=es, abuf=abuf:
                            e.activation(
                                abuf[:, tc * D + es * 512:
                                     tc * D + es * 512 + 512],
                                ps[4 + tc][:, 0:TSL], AF.Sign), inc=1)
                for tc in range(4):
                    b.wait("te", "sc", ev_ac[tc])

            b.wait("ve", None, [tk for tk in hdq if tk])
            b.wait("sy", "te", b.c["te"].n)
            sy_qc = []
            for h in range(NH):
                sy_qc.append(b.dma("sy", a_qcs[:, h * TSL:(h + 1) * TSL],
                                   qtd[h][:, t0:t0 + TSL]))
            for cr in range(CPS):
                c = ts * CPS + cr

                def refs(h, cr=cr):
                    bk, j = h % 4, h // 4
                    stp = ps[bk][:, j * 128:(j + 1) * 128]
                    vc = a_vo[:, cr * D + h * 128:cr * D + h * 128 + 128]
                    kcT = a_kt[:, h * TSL + cr * 128:
                               h * TSL + cr * 128 + 128]
                    kct = a_ko[:, cr * D + h * 128:cr * D + h * 128 + 128]
                    qc_ = a_qcs[:, h * TSL + cr * 128:
                                h * TSL + cr * 128 + 128]
                    return stp, vc, kcT, kct, qc_

                def em_scores(h, c=c):
                    stp, vc, kcT, kct, qc_ = refs(h)
                    sbk = 6 + (h % 2)
                    b.wait("te", None, sy_qc[h])
                    if c > 0:
                        b.wait("ve", "te",
                               st_upd_pe.get(h - 4, None) if h >= 4
                               else st_upd_pe.get(h + 12, None))
                        st_ev_dv[h] = b.op(
                            "ve", lambda e, h=h, stp=stp: e.tensor_copy(
                                a_st[:, h * 128:(h + 1) * 128], stp), inc=1)
                    b.wait("te", "ve", sc_msk_dv[h % 2])
                    pe_sc = b.op("te", lambda e, kcT=kcT, qc_=qc_, sbk=sbk:
                                 e.matmul(ps[sbk][:, 0:128], kcT, qc_,
                                          start=True, stop=True), inc=1)
                    b.wait("ve", "te", pe_sc)
                    sc_msk_dv[h % 2] = b.op(
                        "ve", lambda e, h=h, sbk=sbk: e.tensor_mul(
                            a_sc[h % 2][:, :], ps[sbk][:, 0:128],
                            cb[:, 0:128]), inc=1)

                def em_rest(h, c=c):
                    stp, vc, kcT, kct, qc_ = refs(h)
                    obk = 4 + (h % 2)
                    b.wait("te", "sc", out_ev_ac[h % 2])
                    b.wait("te", "ve", sc_msk_dv[h % 2])
                    if c > 0:
                        b.wait("te", "ve", st_ev_dv[h])
                        b.op("te", lambda e, h=h, qc_=qc_, obk=obk: e.matmul(
                            ps[obk][:, 0:128],
                            a_st[:, h * 128:(h + 1) * 128], qc_,
                            start=True, stop=False))
                    pe_out = b.op(
                        "te", lambda e, vc=vc, h=h, obk=obk, c=c: e.matmul(
                            ps[obk][:, 0:128], vc, a_sc[h % 2][:, :],
                            start=(c == 0), stop=True), inc=1)
                    if h >= 4:
                        b.wait("te", "ve", ccs_dv.get(h - 4))
                    pe_st = b.op(
                        "te", lambda e, kct=kct, vc=vc, stp=stp, c=c, h=h:
                        e.matmul(stp, kct, vc,
                                 start=(c == 0 and h < 4), stop=True), inc=1)
                    st_upd_pe[h] = pe_st
                    b.wait("sc", "te", pe_out)
                    b.wait("sc", "sy", or_sy[h % 2])
                    out_ev_ac[h % 2] = b.op(
                        "sc", lambda e, h=h, obk=obk: e.copy(
                            a_or[h % 2][:, :], ps[obk][:, 0:128]), inc=1)
                    b.wait("sy", "sc", out_ev_ac[h % 2])
                    or_sy[h % 2] = b.dma(
                        "sy", atd[h][:, c * 128:(c + 1) * 128],
                        a_or[h % 2][:, :])

                final = (ts == NSL - 1 and cr == CPS - 1 and PHASE_LIM >= 4)

                def em_ccs(h, final=final):
                    if not final:
                        return
                    bk, j = h % 4, h // 4
                    b.wait("ve", "te", st_upd_pe[h])
                    ccs_dv[h] = b.op(
                        "ve", lambda e, h=h, bk=bk, j=j: e.tensor_copy(
                            a_ccs[:, h * 128:(h + 1) * 128],
                            ps[bk][:, j * 128:(j + 1) * 128]), inc=1)

                em_scores(0)
                for h in range(1, NH):
                    em_scores(h)
                    em_rest(h - 1)
                    em_ccs(h - 1)
                em_rest(NH - 1)
                em_ccs(NH - 1)
                st_ev_dv = {}

        if PHASE_LIM < 4:
            b.gate()
            for kc in range(KC):
                b.dma("sy", yout[kc * 128:(kc + 1) * 128, :], M(kc))
            for tk in b.qsnap("sy"):
                b.wait("sy", None, tk)
            b.finalize(block)
            return nc
        dv_ccs = b.c["ve"].n
        b.wait("sy", "ve", dv_ccs)
        sy_cc = b.dma("sy", ccin.ap()[:, :], a_ccs[:, :])
        b.wait("gp", "sy", sy_cc)
        b.c["cc"].n += 1
        cchandle = b.c["cc"]
        b.th["gp"].append(lambda e, h=cchandle.h: e.collective_compute(
            "AllGather", ALU.bypass,
            ins=[ccin.ap()[:, :]], outs=[ccout.ap()[:, :]],
            replica_groups=[[0, 1], [2, 3], [4, 5], [6, 7]],
        ).then_inc(h, 1))
        b.wait("sy", "cc", 1)
        sy_sin = b.dma("sy", a_ccs[:, :], ccout.ap()[0:128, :])
        b.wait("ve", "sy", sy_sin)
        dv_sin = b.op("ve", lambda e: e.tensor_single_scalar(
            a_sin[:, :], a_ccs[:, :], ct[:, C_FLAG:C_FLAG + 1], ALU.mult),
            inc=1)

        prev_mrg_pe = None
        for ts in (range(NSL) if PHASE_LIM >= 5 else []):
            t0 = ts * TSL
            if ts == 0:
                b.gate()
            else:
                b.wait("sy", "te", prev_mrg_pe)
            sy_q = []
            for h in range(NH):
                sy_q.append(b.dma("sy", a_qf[:, h * TSL:(h + 1) * TSL],
                                  qtd[h][:, t0:t0 + TSL]))
                sy_q.append(b.dma("sy", a_mrg[:, h * TSL:(h + 1) * TSL],
                                  atd[h][:, t0:t0 + TSL]))
            b.wait("te", "ve", dv_sin)
            mrg_dv = [None, None]
            for h in range(NH):
                bank = 4 + (h % 2)
                b.wait("te", None, sy_q[2 * h])
                b.wait("te", None, sy_q[2 * h + 1])
                b.wait("te", "ve", mrg_dv[h % 2])
                pe_h = None
                for c0 in range(CPS):
                    pe_h = b.op(
                        "te", lambda e, h=h, c0=c0, bank=bank: e.matmul(
                            ps[bank][:, c0 * 128:(c0 + 1) * 128],
                            a_sin[:, h * 128:(h + 1) * 128],
                            a_qf[:, h * TSL + c0 * 128:
                                 h * TSL + c0 * 128 + 128],
                            start=(c0 == 0), stop=(c0 == CPS - 1)),
                        inc=(1 if c0 == CPS - 1 else 0))
                b.wait("ve", "te", pe_h)
                mrg_dv[h % 2] = b.op(
                    "ve", lambda e, h=h, bank=bank: e.tensor_add(
                        a_mrg[:, h * TSL:(h + 1) * TSL],
                        a_mrg[:, h * TSL:(h + 1) * TSL],
                        ps[bank][:, 0:TSL]), inc=1)
            b.wait("te", "ve", mrg_dv[0])
            b.wait("te", "ve", mrg_dv[1])
            wp_pe = [None] * 4
            for ep in range(4):
                add_dv = [None] * 4
                for hc in range(NH):
                    wsl = hc % 4
                    b.wait("sy", "te", wp_pe[wsl])
                    sy_w = b.dma(
                        "sy", a_wp[wsl][:, :],
                        wco[hc * 128:(hc + 1) * 128,
                            ep * 512:(ep + 1) * 512], ctr="sq")
                    b.wait("te", None, sy_w)
                    pe_v = None
                    for e4 in range(4):
                        last = (hc == NH - 1)
                        pe_v = b.op(
                            "te", lambda e, wsl=wsl, e4=e4, hc=hc: e.matmul(
                                ps[4 + e4][:, 0:TSL],
                                a_wp[wsl][:, e4 * 128:e4 * 128 + 128],
                                a_mrg[:, hc * TSL:(hc + 1) * TSL],
                                start=(hc == 0), stop=last),
                            inc=(1 if e4 == 3 else 0))
                    wp_pe[wsl] = b.c["te"].n
                for e4 in range(4):
                    e_ = ep * 4 + e4
                    b.wait("ve", "te", wp_pe[(NH - 1) % 4])
                    add_dv[e4] = b.op(
                        "ve", lambda e, e_=e_, e4=e4, t0=t0: e.tensor_add(
                            M(e_)[:, t0:t0 + TSL], M(e_)[:, t0:t0 + TSL],
                            ps[4 + e4][:, 0:TSL]), inc=1)
                for e4 in range(4):
                    b.wait("te", "ve", add_dv[e4])
            prev_mrg_pe = b.c["te"].n

        if DEBUG_TAPS:
            b.gate()
            for kc in range(KC):
                b.dma("sy", taps["x2"][kc * 128:(kc + 1) * 128, :], M(kc))
        if DEBUG_TAPS >= 2:
            for h in range(NH):
                b.dma("sy", taps["dq"][h], qtd[h])
                b.dma("sy", taps["da"][h], atd[h])
            b.dma("sy", taps["dcc"][:, :], ccin.ap()[:, :])
            b.dma("sy", taps["dsin"][:, :], a_sin[:, :])
            b.dma("sy", taps["dwq"][:, :], wcq[:, :])
            b.dma("sy", taps["dwk"][:, :], wck[:, :])

        if PHASE_LIM >= 6:
            emit_ffn(out_stream=(DEBUG_TAPS == 0))

        b.gate()
        if PHASE_LIM < 6 or DEBUG_TAPS:
            for kc in range(KC):
                b.dma("sy", yout[kc * 128:(kc + 1) * 128, :], M(kc))
        for tk in b.qsnap("sy"):
            b.wait("sy", None, tk)

        b.finalize(block)
    return nc


PI_D = _nibble_perm(D)
PI_M = _nibble_perm(MLP)


def _constsi():
    cti = np.zeros((128, 2), dtype=np.int32)
    hi = np.arange(128) >= 64
    cti[:, 0] = np.where(hi, 0x70, 7)
    cti[:, 1] = np.where(hi, 0x80, 8)
    return cti


def _consts(core):
    ct = np.zeros((128, NCONST), dtype=np.float32)
    hi = np.arange(128) >= 64
    ct[:, C_C1] = np.where(hi, 1.0 / 32, 0.5)
    ct[:, C_C2] = np.where(hi, 1.0 / 16, 1.0)
    ct[:, C_CK] = np.where(hi, -1.0 / 64, -0.25)
    ct[:, C_BM2] = -2.0
    ct[:, C_BM6] = -6.0
    ct[:, C_EPS] = 1e-5
    ct[:, C_FLAG] = 1.0 if (core % 2 == 1) else 0.0
    ct[:, C_QS] = np.float32(1.0 / np.sqrt(128.0))
    return ct


def _constsb():
    import ml_dtypes
    cb = np.zeros((128, 256), dtype=np.float32)
    cb[:, 0:128] = np.triu(np.ones((128, 128), dtype=np.float32))
    cb[:, 128:256] = 1.0
    return cb.astype(ml_dtypes.bfloat16)


def kernel(L0_attn_x, L0_attn_Wq, L0_attn_Wk, L0_attn_Wv, L0_attn_Wo,
           L0_ffn_gate_W_packed, L0_ffn_gate_scales, L0_ffn_up_W_packed,
           L0_ffn_up_scales, L0_ffn_down_W_packed, L0_ffn_down_scales):
    x = np.ascontiguousarray(np.asarray(L0_attn_x, dtype=np.float32))
    xt = np.ascontiguousarray(x.reshape(8192 * 2, D).T[PI_D])

    def wT(w):
        return np.ascontiguousarray(np.asarray(w, dtype=np.float32).T[PI_D])

    wq_h = wT(L0_attn_Wq)
    wk_h = wT(L0_attn_Wk)
    wv_h = wT(L0_attn_Wv)
    wo_h = np.ascontiguousarray(
        np.asarray(L0_attn_Wo, dtype=np.float32).T[:, PI_D])

    gp_h = np.ascontiguousarray(
        np.asarray(L0_ffn_gate_W_packed, dtype=np.int32).T[:, PI_M])
    up_h = np.ascontiguousarray(
        np.asarray(L0_ffn_up_W_packed, dtype=np.int32).T[:, PI_M])
    dp_h = np.ascontiguousarray(
        np.asarray(L0_ffn_down_W_packed, dtype=np.int32).T[:, PI_D])
    gs_h = np.ascontiguousarray(
        np.asarray(L0_ffn_gate_scales, dtype=np.float32).T[:, PI_M])
    us_h = np.ascontiguousarray(
        np.asarray(L0_ffn_up_scales, dtype=np.float32).T[:, PI_M])
    ds_h = np.ascontiguousarray(
        np.asarray(L0_ffn_down_scales, dtype=np.float32).T[:, PI_D])
    ctb_h = _constsb()

    nc = build_nc()
    in_maps = []
    for c in range(8):
        in_maps.append({
            "xin": np.ascontiguousarray(xt[:, c * TC:(c + 1) * TC]),
            "wq": wq_h, "wk": wk_h, "wv": wv_h, "wo": wo_h,
            "gp": np.ascontiguousarray(gp_h[:, c * DQN:(c + 1) * DQN]),
            "up": np.ascontiguousarray(up_h[:, c * DQN:(c + 1) * DQN]),
            "dp": np.ascontiguousarray(dp_h[c * 1024:(c + 1) * 1024, :]),
            "gs": np.ascontiguousarray(gs_h[:, c * DQN:(c + 1) * DQN]),
            "us": np.ascontiguousarray(us_h[:, c * DQN:(c + 1) * DQN]),
            "ds": np.ascontiguousarray(ds_h[c * 64:(c + 1) * 64, :]),
            "cts": _consts(c), "ctb": ctb_h, "cti": _constsi(),
        })
    res = run_bass_kernel_spmd(nc, in_maps, core_ids=list(range(8)))

    out = np.empty((D, 8 * TC), dtype=np.float32)
    for c in range(8):
        out[:, c * TC:(c + 1) * TC] = res.results[c]["yout"]
    nat = np.empty_like(out)
    nat[PI_D, :] = out
    if DEBUG_TAPS:
        kernel._taps = []
        if DEBUG_TAPS >= 2:
            kernel._dumps = [
                np.stack([np.asarray(res.results[c]["tap_" + t],
                                     dtype=np.float32) for c in range(8)])
                for t in ("dq", "da", "dcc", "dsin", "dwq", "dwk", "dxn")]
        for t in ("x1", "x2"):
            tp = np.empty((D, 8 * TC), dtype=np.float32)
            for c in range(8):
                tp[:, c * TC:(c + 1) * TC] = res.results[c]["tap_" + t]
            ntp = np.empty_like(tp)
            ntp[PI_D, :] = tp
            kernel._taps.append(ntp.T.reshape(4, 4096, D).copy())
    return np.ascontiguousarray(nat.T).reshape(4, 4096, D)


# version 23 (final)
# speedup vs baseline: 1.0035x; 1.0035x over previous
import os
import numpy as np
from contextlib import ExitStack

import concourse.bass as bass
import concourse.mybir as mybir
from concourse.bass_utils import run_bass_kernel_spmd

F32 = mybir.dt.float32
BF16 = mybir.dt.bfloat16
I16 = mybir.dt.int16
I32 = mybir.dt.int32
AF = mybir.ActivationFunctionType
ALU = mybir.AluOpType

D = 2048
TC = 2048
KC = 16
MLP = 16384
TSL = 512
NSL = TC // TSL
GK2 = 8
NGRP = 128 // GK2
NH = 16
CH = 128
NCH = TC // CH
CPS = TSL // CH
DQN = 2048

NCONST = 16
C_C1, C_C2, C_CK, C_BM2, C_BM6, C_EPS, C_FLAG, C_QS = range(8)

DEBUG_TAPS = int(os.environ.get("ATHENA_DEBUG_TAPS", "0"))
PHASE_LIM = int(os.environ.get("ATHENA_PHASES", "99"))
ENGS = ("sy", "gp", "ve", "sc", "te")


class Ctr:
    def __init__(self, ctx, nc, name):
        self.h = ctx.enter_context(nc.semaphore(name))
        self.n = 0


class B:

    POOL = {"sy": 24, "gp": 12, "sq": 16, "scp": 12}

    def __init__(self, nc, ctx):
        self.nc = nc
        self.th = {q: [] for q in ENGS}
        self.c = {q: Ctr(ctx, nc, "s_" + q) for q in ("ve", "sc", "te")}
        self.c["cc"] = Ctr(ctx, nc, "s_cc")
        self.c["ccw"] = Ctr(ctx, nc, "s_ccw")
        self.pool = {q: [Ctr(ctx, nc, "p_%s%d" % (q, i)) for i in range(n)]
                     for q, n in self.POOL.items()}
        self.rr = {q: 0 for q in self.POOL}

    def dma(self, q, out, in_, ctr=None):
        qq = ctr or q
        pool = self.pool[qq]
        s = pool[self.rr[qq]]
        self.rr[qq] = (self.rr[qq] + 1) % len(pool)
        if s.n > 0:
            self.th[q].append(lambda e, h=s.h, v=s.n: e.wait_ge(h, v))
        s.n += 16
        self.th[q].append(lambda e, o=out, i=in_, h=s.h:
                          e.dma_start(out=o, in_=i).then_inc(h, 16))
        return (s.h, s.n)

    def op(self, q, fn, inc=0):
        if inc:
            c = self.c[q]
            c.n += inc
            self.th[q].append(lambda e, f=fn, h=c.h, k=inc:
                              f(e).then_inc(h, k))
            return c.n
        self.th[q].append(fn)
        return None

    def wait(self, q, src, val):
        if val is None:
            return
        if isinstance(val, tuple):
            self.th[q].append(lambda e, h=val[0], v=val[1]: e.wait_ge(h, v))
            return
        if isinstance(val, list):
            for t_ in val:
                self.wait(q, src, t_)
            return
        if val <= 0:
            return
        self.th[q].append(lambda e, h=self.c[src].h, v=val: e.wait_ge(h, v))

    def qsnap(self, qq):
        return [(s.h, s.n) for s in self.pool[qq] if s.n > 0]

    def gate(self):
        snap = {q: self.c[q].n for q in ("ve", "sc", "te")}
        toks = [t for qq in self.POOL for t in self.qsnap(qq)]
        for q in ENGS:
            for r in ("ve", "sc", "te"):
                if r != q:
                    self.wait(q, r, snap[r])
            self.wait(q, None, toks)

    def finalize(self, block):
        block.sync(lambda e: [t(e) for t in self.th["sy"]])
        block.gpsimd(lambda e: [t(e) for t in self.th["gp"]])
        block.vector(lambda e: [t(e) for t in self.th["ve"]])
        block.scalar(lambda e: [t(e) for t in self.th["sc"]])
        block.tensor(lambda e: [t(e) for t in self.th["te"]])


def _nibble_perm(n):
    nch = n // 128
    pi = np.empty(n, dtype=np.int64)
    idx = np.arange(64)
    for ci in range(nch):
        pi[ci * 128:ci * 128 + 64] = 2 * (64 * ci + idx)
        pi[ci * 128 + 64:ci * 128 + 128] = 2 * (64 * ci + idx) + 1
    return pi


class _PhaseCutExc(Exception):
    pass


_PhaseCut = _PhaseCutExc()


def build_nc():
    nc = bass.Bass("TRN2", target_bir_lowering=False,
                   detect_race_conditions=False)

    xin = nc.dram_tensor("xin", [D, TC], F32, kind="ExternalInput").ap()
    wq = nc.dram_tensor("wq", [D, D], F32, kind="ExternalInput").ap()
    wk = nc.dram_tensor("wk", [D, D], F32, kind="ExternalInput").ap()
    wv = nc.dram_tensor("wv", [D, D], F32, kind="ExternalInput").ap()
    wo = nc.dram_tensor("wo", [D, D], F32, kind="ExternalInput").ap()
    gp_ = nc.dram_tensor("gp", [D // 2, DQN], I32, kind="ExternalInput").ap()
    up_ = nc.dram_tensor("up", [D // 2, DQN], I32, kind="ExternalInput").ap()
    dp_ = nc.dram_tensor("dp", [MLP // 16, D], I32, kind="ExternalInput").ap()
    gs = nc.dram_tensor("gs", [64, DQN], F32, kind="ExternalInput").ap()
    us_ = nc.dram_tensor("us", [64, DQN], F32, kind="ExternalInput").ap()
    ds = nc.dram_tensor("ds", [64, D], F32, kind="ExternalInput").ap()
    cts = nc.dram_tensor("cts", [128, NCONST], F32, kind="ExternalInput").ap()
    cti = nc.dram_tensor("cti", [128, 2], I32, kind="ExternalInput").ap()
    ctb = nc.dram_tensor("ctb", [128, 256], BF16, kind="ExternalInput").ap()
    yout = nc.dram_tensor("yout", [D, TC], F32, kind="ExternalOutput").ap()
    taps = {}
    if DEBUG_TAPS:
        for t in ("x1", "x2"):
            taps[t] = nc.dram_tensor("tap_" + t, [D, TC], F32,
                                     kind="ExternalOutput").ap()
    if DEBUG_TAPS >= 2:
        taps["dq"] = nc.dram_tensor("tap_dq", [NH, 128, TC], BF16,
                                    kind="ExternalOutput").ap()
        taps["da"] = nc.dram_tensor("tap_da", [NH, 128, TC], BF16,
                                    kind="ExternalOutput").ap()
        taps["dcc"] = nc.dram_tensor("tap_dcc", [128, NH * 128], F32,
                                     kind="ExternalOutput").ap()
        taps["dsin"] = nc.dram_tensor("tap_dsin", [128, NH * 128], BF16,
                                      kind="ExternalOutput").ap()
        taps["dwq"] = nc.dram_tensor("tap_dwq", [D, D], BF16,
                                     kind="ExternalOutput").ap()
        taps["dxn"] = nc.dram_tensor("tap_dxn", [128, KC * TSL], BF16,
                                     kind="ExternalOutput").ap()
        taps["dwk"] = nc.dram_tensor("tap_dwk", [D, D], BF16,
                                     kind="ExternalOutput").ap()

    sexg = nc.dram_tensor("sexg", [D, DQN], BF16, kind="Internal").ap()
    sexu = nc.dram_tensor("sexu", [D, DQN], BF16, kind="Internal").ap()
    sexd = nc.dram_tensor("sexd", [MLP // 8, D], BF16, kind="Internal").ap()
    cgs = nc.dram_tensor("cgs", [8, 128, KC, 256], BF16,
                         kind="Internal").ap()
    cus = nc.dram_tensor("cus", [8, 128, KC, 256], BF16,
                         kind="Internal").ap()
    cds_a = nc.dram_tensor("cds_a", [8, 4, 128, 512], BF16,
                           kind="Internal").ap()
    cds_b = nc.dram_tensor("cds_b", [8, 4, 128, 512], BF16,
                           kind="Internal").ap()
    cg = nc.dram_tensor("cg", [64, 128, KC, 256], BF16, kind="Internal",
                        addr_space="Shared").ap()
    cu = nc.dram_tensor("cu", [64, 128, KC, 256], BF16, kind="Internal",
                        addr_space="Shared").ap()
    cd_a = nc.dram_tensor("cd_a", [64, 4, 128, 512], BF16, kind="Internal",
                          addr_space="Shared").ap()
    cd_b = nc.dram_tensor("cd_b", [64, 4, 128, 512], BF16, kind="Internal",
                          addr_space="Shared").ap()
    wcq = nc.dram_tensor("wcq", [D, D], BF16, kind="Internal").ap()
    wck = nc.dram_tensor("wck", [D, D], BF16, kind="Internal").ap()
    wcv = nc.dram_tensor("wcv", [D, D], BF16, kind="Internal").ap()
    wco = nc.dram_tensor("wco", [D, D], BF16, kind="Internal").ap()
    qtd = nc.dram_tensor("qtd", [NH, 128, TC], BF16, kind="Internal").ap()
    atd = nc.dram_tensor("atd", [NH, 128, TC], BF16, kind="Internal").ap()
    ccin = nc.dram_tensor("ccin", [128, NH * 128], F32, kind="Internal")
    ccout = nc.dram_tensor("ccout", [256, NH * 128], F32, kind="Internal")

    ctx = ExitStack()
    with ctx:
        mstr = ctx.enter_context(nc.sbuf_tensor("mstr", [128, KC * TC], F32))
        ct = ctx.enter_context(nc.sbuf_tensor("ct", [128, NCONST], F32))
        cti_t = ctx.enter_context(nc.sbuf_tensor("cti_t", [128, 2], I32))
        cb = ctx.enter_context(nc.sbuf_tensor("cb", [128, 256], BF16))
        base = (nc.lookup_mloc(cb).addr + 512 + 31) & ~31
        LIMIT = 16512 + 212864

        def at(name, shape, dtype, off):
            nb = int(np.prod(shape[1:])) * mybir.dt.size(dtype)
            assert base + off + nb <= LIMIT, (name, base + off + nb, LIMIT)
            return nc.alloc_sbuf_tensor_at(name, shape, dtype,
                                           offset=base + off)

        dq_pt = [at("dq_pt0", [128, DQN], I16, 0),
                 at("dq_pt1", [128, DQN], I16, 4096)]
        dq_sx = [at("dq_sx0", [128, DQN], BF16, 8192),
                 at("dq_sx1", [128, DQN], BF16, 12288)]
        dq_u7 = [at("dq_u70", [128, DQN], I16, 16384),
                 at("dq_u71", [128, DQN], I16, 20480)]
        dq_sb = [at("dq_sb0", [128, DQN], I16, 24576),
                 at("dq_sb1", [128, DQN], I16, 28672)]
        dq_sg2 = [at("dq_sg20", [128, DQN], BF16, 24576),
                  at("dq_sg21", [128, DQN], BF16, 28672)]
        dq_a1 = [at("dq_a10", [128, DQN], BF16, 32768),
                 at("dq_a11", [128, DQN], BF16, 36864)]
        dq_a2 = [at("dq_a20", [128, DQN], BF16, 40960),
                 at("dq_a21", [128, DQN], BF16, 45056)]
        dq_a0 = [at("dq_a00", [128, DQN], BF16, 49152),
                 at("dq_a01", [128, DQN], BF16, 53248)]
        dq_wt = [at("dq_wt0", [128, DQN], BF16, 57344),
                 at("dq_wt1", [128, DQN], BF16, 61440)]
        cvf = at("cvf", [128, D], F32, 65536)
        cvb = at("cvb", [128, D], BF16, 73728)

        f_xn = at("f_xn", [128, KC * TSL], BF16, 0)
        f_int = at("f_int", [128, GK2 * TSL], BF16, 16384)
        f_wgu = [at("f_wgu%d" % i, [128, KC, 256], BF16, 24576 + 8192 * i)
                 for i in range(4)]
        f_wdn = [at("f_wdn%d" % i, [128, 512], BF16, 57344 + 1024 * i)
                 for i in range(16)]
        f_rstd = at("f_rstd", [128, TSL], F32, 73728)
        f_s32a = at("f_s32a", [128, TSL], F32, 75776)
        f_gel = [at("f_gel0", [128, TSL], BF16, 77824),
                 at("f_gel1", [128, TSL], BF16, 78848)]
        f_xb = f_gel[0]
        f_xq = f_gel[1]

        a_xn = f_xn
        a_qcs = at("a_qcs", [128, NH * TSL], BF16, 0)
        a_qf = at("a_qf", [128, NH * TSL], BF16, 0)
        a_kt = at("a_kt", [128, NH * TSL], BF16, 16384)
        a_mrg = at("a_mrg", [128, NH * TSL], BF16, 16384)
        a_ko = at("a_ko", [128, CPS * D], BF16, 32768)
        a_ccs = at("a_ccs", [128, NH * 128], F32, 32768)
        a_sin = at("a_sin", [128, NH * 128], BF16, 40960)
        a_vo = at("a_vo", [128, CPS * D], BF16, 49152)
        a_wp = [at("a_wp%d" % i, [128, 512], BF16, 65536 + 1024 * i)
                for i in range(4)]
        a_tq = [at("a_tq0", [128, TSL], BF16, 69632),
                at("a_tq1", [128, TSL], BF16, 70656),
                at("a_tq2", [128, TSL], BF16, 71680),
                at("a_tq3", [128, TSL], BF16, 72704)]
        a_st = at("a_st", [128, NH * 128], BF16, 71680)
        a_sc = [at("a_sc0", [128, 128], BF16, 75776),
                at("a_sc1", [128, 128], BF16, 76032)]
        a_or = [at("a_or0", [128, 128], BF16, 76288),
                at("a_or1", [128, 128], BF16, 76544)]

        ps = [ctx.enter_context(nc.psum_tensor("psb%d" % i, [128, 512], F32))
              for i in range(8)]

        block = ctx.enter_context(nc.Block())
        b = B(nc, ctx)

        def M(kc):
            return mstr[:, kc * TC:(kc + 1) * TC]

        b.dma("sy", ct[:, :], cts[:, :])
        b.dma("sy", cb[:, :], ctb[:, :])
        b.dma("sy", cti_t[:, :], cti[:, :])
        sy_consts = b.qsnap("sy")
        for kc in range(KC):
            b.dma("sy", M(kc), xin[kc * 128:(kc + 1) * 128, :])
        sy_master = b.qsnap("sy")

        def emit_exp(which, ci):
            dst, srcs, w = ((sexg, gs, DQN), (sexu, us_, DQN),
                            (sexd, ds, D))[which]
            toks = []
            for h2 in range(2):
                d_ = dst[ci * 128 + 64 * h2:ci * 128 + 64 * h2 + 64, :]
                d_ = d_.rearrange("(bb i) n -> bb i n", bb=4, i=16)
                s_ = srcs[4 * ci:4 * ci + 4, :].unsqueeze(1)
                s_ = s_.broadcast_to([4, 16, w])
                toks.append(b.dma("gp", d_, s_))
            return toks
        EXP_JIT = int(os.environ.get("ATHENA_EXP_JIT", "1"))
        gp_exp = None
        if not EXP_JIT:
            for which, nch in ((0, KC), (1, KC), (2, KC)):
                for ci in range(nch):
                    exp_done_pre = emit_exp(which, ci)
            gp_exp = b.qsnap("gp")

        tiles = []
        for mat in (0, 1):
            for kc in range(KC):
                tiles.append(("gu", mat, kc, 0))
        for kc2 in range(KC):
            tiles.append(("dn", kc2, 0, 0))
        NT = len(tiles)

        convs = []
        for (wi, wc_, scol) in ((wq, wcq, C_QS), (wk, wck, None),
                                (wv, wcv, None), (wo, wco, None)):
            for kc in range(KC):
                convs.append((wi, wc_, kc, scol))

        b.wait("gp", "sy", sy_consts)
        b.wait("ve", "sy", sy_consts)
        b.wait("sc", "sy", sy_consts)
        t_wt = [None] * NT
        t_cw = [None] * NT
        conv_ld = [None] * len(convs)
        conv_cv = [None] * len(convs)
        conv_st = [None] * len(convs)
        ci_load = ci_conv = ci_store = 0

        dv_A = [None] * NT
        ac_rel = [None] * NT

        def emit_A(t):
            kind, aa, bb_, cc_ = tiles[t]
            i = t % 2
            b.wait("ve", "gp", gq_after[t])
            b.wait("ve", "sy", sx_after[t])
            b.op("ve", lambda e, i=i: e.tensor_single_scalar(
                dq_u7[i][:, :], dq_pt[i][:, :], cti_t[:, 0:1],
                ALU.bitwise_and))
            dv_A[t] = b.op("ve", lambda e, i=i: e.tensor_single_scalar(
                dq_sb[i][:, :], dq_pt[i][:, :], cti_t[:, 1:2],
                ALU.bitwise_and), inc=1)

        def emit_B(t):
            kind, aa, bb_, cc_ = tiles[t]
            i = t % 2
            b.wait("ve", "sc", ac_rel[t])
            if t >= 2:
                b.wait("ve", "sy", t_cw[t - 2])
            b.op("ve", lambda e, i=i: e.tensor_add(
                dq_a0[i][:, :], dq_a0[i][:, :], dq_a1[i][:, :]))
            b.op("ve", lambda e, i=i: e.tensor_add(
                dq_a0[i][:, :], dq_a0[i][:, :], dq_a2[i][:, :]))
            b.op("ve", lambda e, i=i: e.tensor_scalar(
                dq_sg2[i][:, :], dq_sb[i][:, :], ct[:, C_CK:C_CK + 1], 1.0,
                ALU.mult, ALU.add))
            b.op("ve", lambda e, i=i: e.tensor_mul(
                dq_sg2[i][:, :], dq_sg2[i][:, :], dq_a0[i][:, :]))
            t_wt[t] = b.op("ve", lambda e, i=i: e.tensor_mul(
                dq_wt[i][:, :], dq_sg2[i][:, :], dq_sx[i][:, :]), inc=1)

        def emit_W(t):
            kind, aa, bb_, cc_ = tiles[t]
            i = t % 2
            b.wait("sy", "ve", t_wt[t])
            if kind == "gu":
                mat, kc, ns = aa, bb_, cc_
                cch = cgs if mat == 0 else cus
                p0 = ns * (DQN // 256)
                dst = cch[p0:p0 + 8, :, kc, :].rearrange("q p n -> p q n")
                src2 = dq_wt[i][:, :].rearrange("p (q n) -> p q n", q=8)
                t_cw[t] = b.dma("sy", dst, src2)
            else:
                kc2 = aa
                half, r = (cds_a, kc2) if kc2 < 8 else (cds_b, kc2 - 8)
                t_cw[t] = b.dma(
                    "sy", half[r].rearrange("e p n -> p e n"),
                    dq_wt[i][:, :].rearrange("p (e n) -> p e n", e=4))

        gq_after = [None] * NT
        sx_after = [None] * NT
        exp_done = {}

        def emit_gather(seg_, full_, toks):
            b.wait("gp", None, toks)
            cw = b.c["ccw"]
            cw.n += 1
            b.th["gp"].append(
                lambda e, s=seg_, f=full_, h=cw.h: e.collective_compute(
                    "AllGather", ALU.bypass, ins=[s], outs=[f],
                    replica_groups=[[0, 1, 2, 3, 4, 5, 6, 7]],
                ).then_inc(h, 1))

        for t in range(NT):
            if t == 17:
                emit_gather(cgs, cg, [t_cw[j] for j in range(16)])
            if t == 33:
                emit_gather(cus, cu, [t_cw[j] for j in range(16, 32)])
            if t == 41:
                emit_gather(cds_a, cd_a, [t_cw[j] for j in range(32, 40)])
            kind, aa, bb_, cc_ = tiles[t]
            i = t % 2
            if t >= 2:
                b.wait("gp", "ve", dv_A[t - 2])
            if kind == "gu":
                mat, kc, ns = aa, bb_, cc_
                srcp = gp_ if mat == 0 else up_
                sxs = sexg if mat == 0 else sexu
                if ns == 0:
                    exp_done[(mat, kc)] = (emit_exp(mat, kc) if EXP_JIT
                                           else gp_exp)
                rows = srcp[kc * 64:(kc + 1) * 64, :]
                tp0 = b.dma("gp", dq_pt[i][0:64, :], rows)
                tp1 = b.dma("gp", dq_pt[i][64:128, :], rows)
                if t >= 2:
                    b.wait("sy", "ve", t_wt[t - 2])
                b.wait("sy", "gp", exp_done[(mat, kc)])
                sx_after[t] = b.dma(
                    "sy", dq_sx[i][:, :],
                    sxs[kc * 128:(kc + 1) * 128, :])
            else:
                kc2 = aa
                if kc2 % 16 == 0:
                    for cj in range(kc2, kc2 + 16):
                        exp_done[(2, cj)] = (emit_exp(2, cj) if EXP_JIT
                                             else gp_exp)
                rows = dp_[kc2 * 64:(kc2 + 1) * 64, :]
                tp0 = b.dma("gp", dq_pt[i][0:64, :], rows)
                tp1 = b.dma("gp", dq_pt[i][64:128, :], rows)
                if t >= 2:
                    b.wait("sy", "ve", t_wt[t - 2])
                b.wait("sy", "gp", exp_done[(2, kc2)])
                sx_after[t] = b.dma("sy", dq_sx[i][:, :],
                                    sexd[kc2 * 128:(kc2 + 1) * 128, :])
            key = (mat, kc) if kind == "gu" else (2, aa)
            gq_after[t] = [tp0, tp1] + list(exp_done[key])

            for _ in range(3):
                if ci_conv < ci_load:
                    wi, wc_, kc_c, scol = convs[ci_conv]
                    b.wait("sc", None, conv_ld[ci_conv])
                    if ci_conv >= 1:
                        b.wait("sc", None, conv_st[ci_conv - 1])
                    if scol is None:
                        conv_cv[ci_conv] = b.op(
                            "sc", lambda e: e.copy(cvb[:, :], cvf[:, :]),
                            inc=1)
                    else:
                        conv_cv[ci_conv] = b.op(
                            "sc", lambda e, scol=scol: e.activation(
                                cvb[:, :], cvf[:, :], AF.Copy,
                                scale=ct[:, scol:scol + 1]), inc=1)
                    ci_conv += 1
                if ci_store < ci_conv:
                    wi, wc_, kc_c, scol = convs[ci_store]
                    b.wait("sy", "sc", conv_cv[ci_store])
                    conv_st[ci_store] = b.dma(
                        "sy", wc_[kc_c * 128:(kc_c + 1) * 128, :], cvb[:, :])
                    ci_store += 1
                if ci_load < len(convs) and ci_load <= ci_conv:
                    wi, wc_, kc_c, scol = convs[ci_load]
                    conv_ld[ci_load] = b.dma(
                        "sc", cvf[:, :], wi[kc_c * 128:(kc_c + 1) * 128, :],
                        ctr="scp")
                    ci_load += 1

            emit_A(t)

            b.wait("sc", "ve", dv_A[t])
            b.op("sc", lambda e, i=i: e.activation(
                dq_a0[i][:, :], dq_u7[i][:, :], AF.Relu,
                scale=ct[:, C_C1:C_C1 + 1]))
            b.op("sc", lambda e, i=i: e.activation(
                dq_a1[i][:, :], dq_u7[i][:, :], AF.Relu,
                bias=ct[:, C_BM2:C_BM2 + 1], scale=ct[:, C_C1:C_C1 + 1]))
            ac_rel[t] = b.op("sc", lambda e, i=i: e.activation(
                dq_a2[i][:, :], dq_u7[i][:, :], AF.Relu,
                bias=ct[:, C_BM6:C_BM6 + 1], scale=ct[:, C_C2:C_C2 + 1]),
                inc=1)

            if t >= 1:
                emit_B(t - 1)
                emit_W(t - 1)
        emit_B(NT - 1)
        emit_W(NT - 1)

        emit_gather(cds_b, cd_b, [t_cw[j] for j in range(40, NT)])

        while ci_conv < len(convs):
            wi, wc_, kc_c, scol = convs[ci_conv]
            b.wait("sc", None, conv_ld[ci_conv])
            b.wait("sc", None, conv_st[ci_conv - 1])
            if scol is None:
                conv_cv[ci_conv] = b.op(
                    "sc", lambda e: e.copy(cvb[:, :], cvf[:, :]), inc=1)
            else:
                conv_cv[ci_conv] = b.op(
                    "sc", lambda e, scol=scol: e.activation(
                        cvb[:, :], cvf[:, :], AF.Copy,
                        scale=ct[:, scol:scol + 1]), inc=1)
            ci_conv += 1
            b.wait("sy", "sc", conv_cv[ci_store])
            conv_st[ci_store] = b.dma(
                "sy", wc_[kc_c * 128:(kc_c + 1) * 128, :], cvb[:, :])
            ci_store += 1
        while ci_store < len(convs):
            wi, wc_, kc_c, scol = convs[ci_store]
            b.wait("sy", "sc", conv_cv[ci_store])
            conv_st[ci_store] = b.dma(
                "sy", wc_[kc_c * 128:(kc_c + 1) * 128, :], cvb[:, :])
            ci_store += 1
        assert ci_load == len(convs), ci_load

        b.wait("ve", "sy", sy_master)
        b.gate()

        def emit_ln(ts, xn_buf, sq_eng="sc"):
            t0 = ts * TSL
            pe_k = [None] * KC
            for kc in range(KC):
                if sq_eng == "sc":
                    xb_, xq_ = f_xb, f_xq
                    if kc >= 1:
                        b.wait("sc", "te", pe_k[kc - 1])
                    b.op("sc", lambda e, kc=kc, t0=t0, xb_=xb_: e.copy(
                        xb_[:, :], M(kc)[:, t0:t0 + TSL]))
                    ac_sq = b.op("sc", lambda e, xb_=xb_, xq_=xq_:
                                 e.activation(xq_[:, :], xb_[:, :],
                                              AF.Square), inc=1)
                else:
                    xb_ = f_xb if kc % 2 == 0 else a_tq[2]
                    xq_ = f_xq if kc % 2 == 0 else a_tq[3]
                    if kc >= 2:
                        b.wait("ve", "te", pe_k[kc - 2])
                    b.op("ve", lambda e, kc=kc, t0=t0, xb_=xb_:
                         e.tensor_copy(xb_[:, :], M(kc)[:, t0:t0 + TSL]))
                    ac_sq = b.op("ve", lambda e, xb_=xb_, xq_=xq_:
                                 e.tensor_mul(xq_[:, :], xb_[:, :],
                                              xb_[:, :]), inc=1)
                b.wait("te", sq_eng, ac_sq)
                b.op("te", lambda e, kc=kc, xb_=xb_: e.matmul(
                    ps[6][:, 0:TSL], cb[:, 128:256], xb_[:, :],
                    start=(kc == 0), stop=(kc == KC - 1)))
                pe_k[kc] = b.op("te", lambda e, kc=kc, xq_=xq_: e.matmul(
                    ps[7][:, 0:TSL], cb[:, 128:256], xq_[:, :],
                    start=(kc == 0), stop=(kc == KC - 1)), inc=1)
            pe_last = pe_k[-1]

            b.wait("ve", "te", pe_last)
            b.op("ve", lambda e: e.tensor_single_scalar(
                f_rstd[:, :], ps[6][:, 0:TSL], 1.0 / D, ALU.mult))
            b.op("ve", lambda e: e.tensor_single_scalar(
                f_s32a[:, :], ps[7][:, 0:TSL], 1.0 / D, ALU.mult))
            b.op("ve", lambda e: e.tensor_mul(
                f_rstd[:, :], f_rstd[:, :], f_rstd[:, :]))
            dv_var = b.op("ve", lambda e: e.tensor_sub(
                f_s32a[:, :], f_s32a[:, :], f_rstd[:, :]), inc=1)
            b.wait("sc", "ve", dv_var)
            ac_sd = b.op("sc", lambda e: e.activation(
                f_rstd[:, :], f_s32a[:, :], AF.Sqrt,
                bias=ct[:, C_EPS:C_EPS + 1]), inc=1)
            b.wait("ve", "sc", ac_sd)
            b.op("ve", lambda e: e.reciprocal(f_rstd[:, :], f_rstd[:, :]))
            dv_xn_k = []
            for kc in range(KC):
                b.op("ve", lambda e, kc=kc, t0=t0: e.scalar_tensor_tensor(
                    f_s32a[:, :], ps[6][:, 0:TSL], -1.0 / D,
                    M(kc)[:, t0:t0 + TSL], ALU.mult, ALU.add))
                dv_xn_k.append(b.op(
                    "ve", lambda e, kc=kc, xb=xn_buf: e.tensor_mul(
                        xb[:, kc * TSL:(kc + 1) * TSL], f_s32a[:, :],
                        f_rstd[:, :]), inc=1))
            return dv_xn_k

        def emit_ffn(dn_gate=False, out_stream=False):
            dn_slot = [0]
            dn_used = [None] * 16
            gel_ac = [None, None]
            int_dv = [None, None]
            gu_pe = [None] * 4
            dn_bank_dv = [None, None]
            pe_dn_all = None
            for ts in range(NSL):
                t0 = ts * TSL
                if ts == 0:
                    b.gate()
                else:
                    b.wait("sc", "ve", int_dv[0])
                    b.wait("sc", "ve", int_dv[1])
                dv_xn = emit_ln(ts, f_xn)

                for g in range(NGRP):
                    for pr in range(4):
                        p = g * 4 + pr
                        sl = (p % 2) * 2
                        b.wait("sy", "te", gu_pe[sl])
                        tk_g = b.dma("sy", f_wgu[sl][:, :, :], cg[p])
                        b.wait("gp", "te", gu_pe[sl + 1])
                        tk_u = b.dma("gp", f_wgu[sl + 1][:, :, :], cu[p])
                        sy_pair = [tk_g, tk_u]

                        b.wait("te", "sy", sy_pair)
                        for half in range(2):
                            for sub in range(2):
                                bank = half * 2 + sub
                                if half == 0:
                                    b.wait("te", "sc", gel_ac[sub])
                                else:
                                    b.wait("te", "ve", int_dv[sub])
                                pe_v = None
                                for kc in range(KC):
                                    if g == 0 and pr == 0 and half == 0 \
                                            and sub == 0:
                                        b.wait("te", "ve", dv_xn[kc])
                                    last = (kc == KC - 1)
                                    pe_v = b.op(
                                        "te",
                                        lambda e, hf=half, sb2=sub, kc=kc,
                                        bank=bank, sl=sl: e.matmul(
                                            ps[bank][:, 0:TSL],
                                            f_wgu[sl + hf][
                                                :, kc,
                                                sb2 * 128:sb2 * 128 + 128],
                                            f_xn[:,
                                                 kc * TSL:(kc + 1) * TSL],
                                            start=(kc == 0), stop=last),
                                        inc=(1 if last else 0))
                                if half == 0:
                                    b.wait("sc", "te", pe_v)
                                    gel_ac[sub] = b.op(
                                        "sc", lambda e, sb2=sub: e.activation(
                                            f_gel[sb2][:, :],
                                            ps[sb2][:, 0:TSL],
                                            AF.Gelu_apprx_tanh), inc=1)
                                else:
                                    b.wait("ve", "te", pe_v)
                                    b.wait("ve", "sc", gel_ac[sub])
                                    if pe_dn_all is not None and g >= 1 \
                                            and pr == 0:
                                        b.wait("ve", "te", pe_dn_all)
                                    kk = pr * 2 + sub
                                    int_dv[sub] = b.op(
                                        "ve", lambda e, sb2=sub, kk=kk:
                                        e.tensor_mul(
                                            f_int[:, kk * TSL:(kk + 1) * TSL],
                                            f_gel[sb2][:, :],
                                            ps[2 + sb2][:, 0:TSL]), inc=1)
                            gu_pe[sl + half] = b.c["te"].n
                    b.wait("te", "ve", int_dv[1])
                    if dn_gate and ts == 0 and g <= 1:
                        b.wait("sc", "ccw", 3 + g)
                    qslots = [None] * 4
                    for eq in range(4):
                        qslots[eq] = []
                        dn_toks = []
                        cdh = cd_a if g % 2 == 0 else cd_b
                        rb = (g // 2) * 8
                        for j in range(GK2):
                            slq = dn_slot[0]
                            b.wait("sc", "te", dn_used[slq])
                            dn_toks.append(b.dma(
                                "sc", f_wdn[slq][:, :],
                                cdh[rb + j, eq], ctr="scp"))
                            qslots[eq].append(slq)
                            dn_slot[0] = (slq + 1) % 16
                        b.wait("te", None, dn_toks)
                        for e4 in range(4):
                            e = eq * 4 + e4
                            bank = 4 + (e % 2)
                            b.wait("te", "ve", dn_bank_dv[e % 2])
                            pe_e = None
                            for j in range(GK2):
                                slq = qslots[eq][j]
                                last = (j == GK2 - 1)
                                pe_e = b.op(
                                    "te", lambda e_, slq=slq, e4=e4, j=j,
                                    bank=bank: e_.matmul(
                                        ps[bank][:, 0:TSL],
                                        f_wdn[slq][:, e4 * 128:
                                                   e4 * 128 + 128],
                                        f_int[:, j * TSL:(j + 1) * TSL],
                                        start=(j == 0), stop=last),
                                    inc=(1 if last else 0))
                            if e4 == 3:
                                for slq in qslots[eq]:
                                    dn_used[slq] = pe_e
                            b.wait("ve", "te", pe_e)
                            dn_bank_dv[e % 2] = b.op(
                                "ve", lambda e_, e=e, t0=t0, bank=bank:
                                e_.scalar_tensor_tensor(
                                    M(e)[:, t0:t0 + TSL], ps[bank][:, 0:TSL],
                                    0.5, M(e)[:, t0:t0 + TSL],
                                    ALU.mult, ALU.add), inc=1)
                    pe_dn_all = b.c["te"].n
                if out_stream:
                    b.wait("sy", "ve", dn_bank_dv[0])
                    b.wait("sy", "ve", dn_bank_dv[1])
                    for kc in range(KC):
                        b.dma("sy", yout[kc * 128:(kc + 1) * 128,
                                         t0:t0 + TSL],
                              M(kc)[:, t0:t0 + TSL])

        b.wait("sy", "ccw", 2)
        b.wait("gp", "ccw", 2)
        if PHASE_LIM >= 2:
            emit_ffn(dn_gate=True)
        if DEBUG_TAPS:
            b.gate()
            for kc in range(KC):
                b.dma("sy", taps["x1"][kc * 128:(kc + 1) * 128, :], M(kc))

        hdq = [None, None, None, None]
        wp_pe = [None] * 4
        st_upd_pe = {}
        st_ev_dv = {}
        sc_msk_dv = [None, None]
        out_ev_ac = [None, None]
        or_sy = [None, None]
        ccs_dv = {}
        for ts in (range(NSL) if PHASE_LIM >= 3 else []):
            t0 = ts * TSL
            if ts == 0:
                b.gate()
            b.wait("ve", None, [tk for tk in hdq if tk])
            dv_xn = emit_ln(ts, a_xn, sq_eng="ve")
            if DEBUG_TAPS >= 2 and ts == 0:
                b.wait("sy", "ve", dv_xn[-1])
                b.dma("sy", taps["dxn"][:, :], a_xn[:, :])

            b.wait("te", "ve", dv_xn[-1])
            for (wc_, is_q) in ((wcq, True), (wck, False)):
                ev_ac = [None] * 4
                for ep in range(4):
                    for kc in range(KC):
                        wsl = (ep * KC + kc) % 4
                        b.wait("sy", "te", wp_pe[wsl])
                        sy_w = b.dma(
                            "sy", a_wp[wsl][:, :],
                            wc_[kc * 128:(kc + 1) * 128,
                                ep * 512:(ep + 1) * 512], ctr="sq")
                        b.wait("te", None, sy_w)
                        if kc == 0:
                            for e4 in range(4):
                                b.wait("te", "sc", ev_ac[e4])
                        pe_v = None
                        for e4 in range(4):
                            last = (kc == KC - 1)
                            pe_v = b.op(
                                "te", lambda e, wsl=wsl, e4=e4, kc=kc:
                                e.matmul(
                                    ps[4 + e4][:, 0:TSL],
                                    a_wp[wsl][:, e4 * 128:e4 * 128 + 128],
                                    a_xn[:, kc * TSL:(kc + 1) * TSL],
                                    start=(kc == 0), stop=last),
                                inc=(1 if e4 == 3 else 0))
                        wp_pe[wsl] = b.c["te"].n
                    for e4 in range(4):
                        h = ep * 4 + e4
                        b.wait("sc", "te", b.c["te"].n)
                        if is_q:
                            tq = e4 % 4
                            b.wait("sc", "sy", hdq[tq])
                            ev_ac[e4] = b.op(
                                "sc", lambda e, tq=tq, e4=e4: e.copy(
                                    a_tq[tq][:, :], ps[4 + e4][:, 0:TSL]),
                                inc=1)
                            b.wait("sy", "sc", ev_ac[e4])
                            hdq[tq] = b.dma(
                                "sy", qtd[h][:, t0:t0 + TSL],
                                a_tq[tq][:, :])
                        else:
                            ev_ac[e4] = b.op(
                                "sc", lambda e, h=h, e4=e4: e.activation(
                                    a_kt[:, h * TSL:(h + 1) * TSL],
                                    ps[4 + e4][:, 0:TSL], AF.Sign), inc=1)
                for e4 in range(4):
                    b.wait("te", "sc", ev_ac[e4])

            for (wc_, abuf) in ((wck, a_ko), (wcv, a_vo)):
                ev_ac = [None] * 4
                for es in range(4):
                    for kc in range(KC):
                        wsl = (es * KC + kc) % 4
                        b.wait("sy", "te", wp_pe[wsl])
                        sy_w = b.dma(
                            "sy", a_wp[wsl][:, :],
                            wc_[kc * 128:(kc + 1) * 128,
                                es * 512:(es + 1) * 512], ctr="sq")
                        b.wait("te", None, sy_w)
                        if kc == 0:
                            for tc in range(4):
                                b.wait("te", "sc", ev_ac[tc])
                        pe_v = None
                        for tc in range(4):
                            last = (kc == KC - 1)
                            pe_v = b.op(
                                "te", lambda e, wsl=wsl, tc=tc, kc=kc:
                                e.matmul(
                                    ps[4 + tc][:, 0:TSL],
                                    a_xn[:, kc * TSL + tc * 128:
                                         kc * TSL + tc * 128 + 128],
                                    a_wp[wsl][:, :],
                                    start=(kc == 0), stop=last),
                                inc=(1 if tc == 3 else 0))
                        wp_pe[wsl] = b.c["te"].n
                    for tc in range(4):
                        b.wait("sc", "te", b.c["te"].n)
                        ev_ac[tc] = b.op(
                            "sc", lambda e, tc=tc, es=es, abuf=abuf:
                            e.activation(
                                abuf[:, tc * D + es * 512:
                                     tc * D + es * 512 + 512],
                                ps[4 + tc][:, 0:TSL], AF.Sign), inc=1)
                for tc in range(4):
                    b.wait("te", "sc", ev_ac[tc])

            b.wait("ve", None, [tk for tk in hdq if tk])
            b.wait("sy", "te", b.c["te"].n)
            sy_qc = []
            for h in range(NH):
                sy_qc.append(b.dma("sy", a_qcs[:, h * TSL:(h + 1) * TSL],
                                   qtd[h][:, t0:t0 + TSL]))
            for cr in range(CPS):
                c = ts * CPS + cr

                def refs(h, cr=cr):
                    bk, j = h % 4, h // 4
                    stp = ps[bk][:, j * 128:(j + 1) * 128]
                    vc = a_vo[:, cr * D + h * 128:cr * D + h * 128 + 128]
                    kcT = a_kt[:, h * TSL + cr * 128:
                               h * TSL + cr * 128 + 128]
                    kct = a_ko[:, cr * D + h * 128:cr * D + h * 128 + 128]
                    qc_ = a_qcs[:, h * TSL + cr * 128:
                                h * TSL + cr * 128 + 128]
                    return stp, vc, kcT, kct, qc_

                def em_scores(h, c=c):
                    stp, vc, kcT, kct, qc_ = refs(h)
                    sbk = 6 + (h % 2)
                    b.wait("te", None, sy_qc[h])
                    if c > 0:
                        b.wait("ve", "te",
                               st_upd_pe.get(h - 4, None) if h >= 4
                               else st_upd_pe.get(h + 12, None))
                        st_ev_dv[h] = b.op(
                            "ve", lambda e, h=h, stp=stp: e.tensor_copy(
                                a_st[:, h * 128:(h + 1) * 128], stp), inc=1)
                    b.wait("te", "ve", sc_msk_dv[h % 2])
                    pe_sc = b.op("te", lambda e, kcT=kcT, qc_=qc_, sbk=sbk:
                                 e.matmul(ps[sbk][:, 0:128], kcT, qc_,
                                          start=True, stop=True), inc=1)
                    b.wait("ve", "te", pe_sc)
                    sc_msk_dv[h % 2] = b.op(
                        "ve", lambda e, h=h, sbk=sbk: e.tensor_mul(
                            a_sc[h % 2][:, :], ps[sbk][:, 0:128],
                            cb[:, 0:128]), inc=1)

                def em_rest(h, c=c):
                    stp, vc, kcT, kct, qc_ = refs(h)
                    obk = 4 + (h % 2)
                    b.wait("te", "sc", out_ev_ac[h % 2])
                    b.wait("te", "ve", sc_msk_dv[h % 2])
                    if c > 0:
                        b.wait("te", "ve", st_ev_dv[h])
                        b.op("te", lambda e, h=h, qc_=qc_, obk=obk: e.matmul(
                            ps[obk][:, 0:128],
                            a_st[:, h * 128:(h + 1) * 128], qc_,
                            start=True, stop=False))
                    pe_out = b.op(
                        "te", lambda e, vc=vc, h=h, obk=obk, c=c: e.matmul(
                            ps[obk][:, 0:128], vc, a_sc[h % 2][:, :],
                            start=(c == 0), stop=True), inc=1)
                    if h >= 4:
                        b.wait("te", "ve", ccs_dv.get(h - 4))
                    pe_st = b.op(
                        "te", lambda e, kct=kct, vc=vc, stp=stp, c=c, h=h:
                        e.matmul(stp, kct, vc,
                                 start=(c == 0 and h < 4), stop=True), inc=1)
                    st_upd_pe[h] = pe_st
                    b.wait("sc", "te", pe_out)
                    b.wait("sc", "sy", or_sy[h % 2])
                    out_ev_ac[h % 2] = b.op(
                        "sc", lambda e, h=h, obk=obk: e.copy(
                            a_or[h % 2][:, :], ps[obk][:, 0:128]), inc=1)
                    b.wait("sy", "sc", out_ev_ac[h % 2])
                    or_sy[h % 2] = b.dma(
                        "sy", atd[h][:, c * 128:(c + 1) * 128],
                        a_or[h % 2][:, :])

                final = (ts == NSL - 1 and cr == CPS - 1 and PHASE_LIM >= 4)

                def em_ccs(h, final=final):
                    if not final:
                        return
                    bk, j = h % 4, h // 4
                    b.wait("ve", "te", st_upd_pe[h])
                    ccs_dv[h] = b.op(
                        "ve", lambda e, h=h, bk=bk, j=j: e.tensor_copy(
                            a_ccs[:, h * 128:(h + 1) * 128],
                            ps[bk][:, j * 128:(j + 1) * 128]), inc=1)

                em_scores(0)
                for h in range(1, NH):
                    em_scores(h)
                    em_rest(h - 1)
                    em_ccs(h - 1)
                em_rest(NH - 1)
                em_ccs(NH - 1)
                st_ev_dv = {}

        if PHASE_LIM < 4:
            b.gate()
            for kc in range(KC):
                b.dma("sy", yout[kc * 128:(kc + 1) * 128, :], M(kc))
            for tk in b.qsnap("sy"):
                b.wait("sy", None, tk)
            b.finalize(block)
            return nc
        dv_ccs = b.c["ve"].n
        b.wait("sy", "ve", dv_ccs)
        sy_cc = b.dma("sy", ccin.ap()[:, :], a_ccs[:, :])
        b.wait("gp", "sy", sy_cc)
        b.c["cc"].n += 1
        cchandle = b.c["cc"]
        b.th["gp"].append(lambda e, h=cchandle.h: e.collective_compute(
            "AllGather", ALU.bypass,
            ins=[ccin.ap()[:, :]], outs=[ccout.ap()[:, :]],
            replica_groups=[[0, 1], [2, 3], [4, 5], [6, 7]],
        ).then_inc(h, 1))
        b.wait("sy", "cc", 1)
        sy_sin = b.dma("sy", a_ccs[:, :], ccout.ap()[0:128, :])
        b.wait("ve", "sy", sy_sin)
        dv_sin = b.op("ve", lambda e: e.tensor_single_scalar(
            a_sin[:, :], a_ccs[:, :], ct[:, C_FLAG:C_FLAG + 1], ALU.mult),
            inc=1)

        prev_mrg_pe = None
        for ts in (range(NSL) if PHASE_LIM >= 5 else []):
            t0 = ts * TSL
            if ts == 0:
                b.gate()
            else:
                b.wait("sy", "te", prev_mrg_pe)
            sy_q = []
            for h in range(NH):
                sy_q.append(b.dma("sy", a_qf[:, h * TSL:(h + 1) * TSL],
                                  qtd[h][:, t0:t0 + TSL]))
                sy_q.append(b.dma("sy", a_mrg[:, h * TSL:(h + 1) * TSL],
                                  atd[h][:, t0:t0 + TSL]))
            b.wait("te", "ve", dv_sin)
            mrg_dv = [None, None]
            for h in range(NH):
                bank = 4 + (h % 2)
                b.wait("te", None, sy_q[2 * h])
                b.wait("te", None, sy_q[2 * h + 1])
                b.wait("te", "ve", mrg_dv[h % 2])
                pe_h = None
                for c0 in range(CPS):
                    pe_h = b.op(
                        "te", lambda e, h=h, c0=c0, bank=bank: e.matmul(
                            ps[bank][:, c0 * 128:(c0 + 1) * 128],
                            a_sin[:, h * 128:(h + 1) * 128],
                            a_qf[:, h * TSL + c0 * 128:
                                 h * TSL + c0 * 128 + 128],
                            start=(c0 == 0), stop=(c0 == CPS - 1)),
                        inc=(1 if c0 == CPS - 1 else 0))
                b.wait("ve", "te", pe_h)
                mrg_dv[h % 2] = b.op(
                    "ve", lambda e, h=h, bank=bank: e.tensor_add(
                        a_mrg[:, h * TSL:(h + 1) * TSL],
                        a_mrg[:, h * TSL:(h + 1) * TSL],
                        ps[bank][:, 0:TSL]), inc=1)
            b.wait("te", "ve", mrg_dv[0])
            b.wait("te", "ve", mrg_dv[1])
            wp_pe = [None] * 4
            for ep in range(4):
                add_dv = [None] * 4
                for hc in range(NH):
                    wsl = hc % 4
                    b.wait("sy", "te", wp_pe[wsl])
                    sy_w = b.dma(
                        "sy", a_wp[wsl][:, :],
                        wco[hc * 128:(hc + 1) * 128,
                            ep * 512:(ep + 1) * 512], ctr="sq")
                    b.wait("te", None, sy_w)
                    pe_v = None
                    for e4 in range(4):
                        last = (hc == NH - 1)
                        pe_v = b.op(
                            "te", lambda e, wsl=wsl, e4=e4, hc=hc: e.matmul(
                                ps[4 + e4][:, 0:TSL],
                                a_wp[wsl][:, e4 * 128:e4 * 128 + 128],
                                a_mrg[:, hc * TSL:(hc + 1) * TSL],
                                start=(hc == 0), stop=last),
                            inc=(1 if e4 == 3 else 0))
                    wp_pe[wsl] = b.c["te"].n
                for e4 in range(4):
                    e_ = ep * 4 + e4
                    b.wait("ve", "te", wp_pe[(NH - 1) % 4])
                    add_dv[e4] = b.op(
                        "ve", lambda e, e_=e_, e4=e4, t0=t0: e.tensor_add(
                            M(e_)[:, t0:t0 + TSL], M(e_)[:, t0:t0 + TSL],
                            ps[4 + e4][:, 0:TSL]), inc=1)
                for e4 in range(4):
                    b.wait("te", "ve", add_dv[e4])
            prev_mrg_pe = b.c["te"].n

        if DEBUG_TAPS:
            b.gate()
            for kc in range(KC):
                b.dma("sy", taps["x2"][kc * 128:(kc + 1) * 128, :], M(kc))
        if DEBUG_TAPS >= 2:
            for h in range(NH):
                b.dma("sy", taps["dq"][h], qtd[h])
                b.dma("sy", taps["da"][h], atd[h])
            b.dma("sy", taps["dcc"][:, :], ccin.ap()[:, :])
            b.dma("sy", taps["dsin"][:, :], a_sin[:, :])
            b.dma("sy", taps["dwq"][:, :], wcq[:, :])
            b.dma("sy", taps["dwk"][:, :], wck[:, :])

        if PHASE_LIM >= 6:
            emit_ffn(out_stream=(DEBUG_TAPS == 0))

        b.gate()
        if PHASE_LIM < 6 or DEBUG_TAPS:
            for kc in range(KC):
                b.dma("sy", yout[kc * 128:(kc + 1) * 128, :], M(kc))
        for tk in b.qsnap("sy"):
            b.wait("sy", None, tk)

        b.finalize(block)
    return nc


PI_D = _nibble_perm(D)
PI_M = _nibble_perm(MLP)


def _constsi():
    cti = np.zeros((128, 2), dtype=np.int32)
    hi = np.arange(128) >= 64
    cti[:, 0] = np.where(hi, 0x70, 7)
    cti[:, 1] = np.where(hi, 0x80, 8)
    return cti


def _consts(core):
    ct = np.zeros((128, NCONST), dtype=np.float32)
    hi = np.arange(128) >= 64
    ct[:, C_C1] = np.where(hi, 1.0 / 32, 0.5)
    ct[:, C_C2] = np.where(hi, 1.0 / 16, 1.0)
    ct[:, C_CK] = np.where(hi, -1.0 / 64, -0.25)
    ct[:, C_BM2] = -2.0
    ct[:, C_BM6] = -6.0
    ct[:, C_EPS] = 1e-5
    ct[:, C_FLAG] = 1.0 if (core % 2 == 1) else 0.0
    ct[:, C_QS] = np.float32(1.0 / np.sqrt(128.0))
    return ct


def _constsb():
    import ml_dtypes
    cb = np.zeros((128, 256), dtype=np.float32)
    cb[:, 0:128] = np.triu(np.ones((128, 128), dtype=np.float32))
    cb[:, 128:256] = 1.0
    return cb.astype(ml_dtypes.bfloat16)


def kernel(L0_attn_x, L0_attn_Wq, L0_attn_Wk, L0_attn_Wv, L0_attn_Wo,
           L0_ffn_gate_W_packed, L0_ffn_gate_scales, L0_ffn_up_W_packed,
           L0_ffn_up_scales, L0_ffn_down_W_packed, L0_ffn_down_scales):
    x = np.ascontiguousarray(np.asarray(L0_attn_x, dtype=np.float32))
    xt = np.ascontiguousarray(x.reshape(8192 * 2, D).T[PI_D])

    def wT(w):
        return np.ascontiguousarray(np.asarray(w, dtype=np.float32).T[PI_D])

    wq_h = wT(L0_attn_Wq)
    wk_h = wT(L0_attn_Wk)
    wv_h = wT(L0_attn_Wv)
    wo_h = np.ascontiguousarray(
        np.asarray(L0_attn_Wo, dtype=np.float32).T[:, PI_D])

    gp_h = np.ascontiguousarray(
        np.asarray(L0_ffn_gate_W_packed, dtype=np.int32).T[:, PI_M])
    up_h = np.ascontiguousarray(
        np.asarray(L0_ffn_up_W_packed, dtype=np.int32).T[:, PI_M])
    dp_h = np.ascontiguousarray(
        np.asarray(L0_ffn_down_W_packed, dtype=np.int32).T[:, PI_D])
    gs_h = np.ascontiguousarray(
        np.asarray(L0_ffn_gate_scales, dtype=np.float32).T[:, PI_M])
    us_h = np.ascontiguousarray(
        np.asarray(L0_ffn_up_scales, dtype=np.float32).T[:, PI_M])
    ds_h = np.ascontiguousarray(
        np.asarray(L0_ffn_down_scales, dtype=np.float32).T[:, PI_D])
    ctb_h = _constsb()

    nc = build_nc()
    in_maps = []
    for c in range(8):
        in_maps.append({
            "xin": np.ascontiguousarray(xt[:, c * TC:(c + 1) * TC]),
            "wq": wq_h, "wk": wk_h, "wv": wv_h, "wo": wo_h,
            "gp": np.ascontiguousarray(gp_h[:, c * DQN:(c + 1) * DQN]),
            "up": np.ascontiguousarray(up_h[:, c * DQN:(c + 1) * DQN]),
            "dp": np.ascontiguousarray(dp_h[c * 1024:(c + 1) * 1024, :]),
            "gs": np.ascontiguousarray(gs_h[:, c * DQN:(c + 1) * DQN]),
            "us": np.ascontiguousarray(us_h[:, c * DQN:(c + 1) * DQN]),
            "ds": np.ascontiguousarray(ds_h[c * 64:(c + 1) * 64, :]),
            "cts": _consts(c), "ctb": ctb_h, "cti": _constsi(),
        })
    res = run_bass_kernel_spmd(nc, in_maps, core_ids=list(range(8)))

    out = np.empty((D, 8 * TC), dtype=np.float32)
    for c in range(8):
        out[:, c * TC:(c + 1) * TC] = res.results[c]["yout"]
    nat = np.empty_like(out)
    nat[PI_D, :] = out
    if DEBUG_TAPS:
        kernel._taps = []
        if DEBUG_TAPS >= 2:
            kernel._dumps = [
                np.stack([np.asarray(res.results[c]["tap_" + t],
                                     dtype=np.float32) for c in range(8)])
                for t in ("dq", "da", "dcc", "dsin", "dwq", "dwk", "dxn")]
        for t in ("x1", "x2"):
            tp = np.empty((D, 8 * TC), dtype=np.float32)
            for c in range(8):
                tp[:, c * TC:(c + 1) * TC] = res.results[c]["tap_" + t]
            ntp = np.empty_like(tp)
            ntp[PI_D, :] = tp
            kernel._taps.append(ntp.T.reshape(4, 4096, D).copy())
    return np.ascontiguousarray(nat.T).reshape(4, 4096, D)
